# version 01
# baseline (speedup 1.0000x reference)
import numpy as np
from contextlib import ExitStack

import concourse.bass as bass
import concourse.mybir as mybir
from concourse.bass_utils import run_bass_kernel_spmd

F32 = mybir.dt.float32
BF16 = mybir.dt.bfloat16
AF = mybir.ActivationFunctionType
ALU = mybir.AluOpType

D = 2048
DI = 4096
NH = 32
HD = 128
DS = 64
CH = 128
EPS = 1e-5
CTX = 3
KD = D // 128
DEV_DUMPS = False


def ceil_div(a, b):
    return (a + b - 1) // b


class Ctr:
    def __init__(self, xs, nc, name, step=1):
        self.h = xs.enter_context(nc.semaphore(name))
        self.n = 0
        self.step = step

    def inc(self, instr, k=1):
        instr.then_inc(self.h, k * self.step)
        self.n += k * self.step
        return self.n


def build(T, n_cores):
    assert T % 512 == 0
    TE = T + CTX
    NCH = T // CH
    NT = ceil_div(TE, 128)
    NTR = T // 128
    n512 = T // 512
    NCb = n_cores

    nc = bass.Bass("TRN2", target_bir_lowering=False, debug=False,
                   detect_race_conditions=False)

    u_in = nc.dram_tensor("u_in", [TE, D], F32, kind="ExternalInput").ap()
    w_z = nc.dram_tensor("w_z", [NH, 128, KD * 128], F32, kind="ExternalInput").ap()
    w_x = nc.dram_tensor("w_x", [NH, 128, KD * 128], F32, kind="ExternalInput").ap()
    w_b = nc.dram_tensor("w_b", [128, KD * DS], F32, kind="ExternalInput").ap()
    w_c = nc.dram_tensor("w_c", [128, KD * DS], F32, kind="ExternalInput").ap()
    w_dt = nc.dram_tensor("w_dt", [128, KD * NH], F32, kind="ExternalInput").ap()
    w_out = nc.dram_tensor("w_out", [KD, 128, 32 * 128], F32, kind="ExternalInput").ap()
    w_up = nc.dram_tensor("w_up", [KD, 128, KD * 128], F32, kind="ExternalInput").ap()
    w_dn = nc.dram_tensor("w_dn", [KD, 128, KD * 128], F32, kind="ExternalInput").ap()
    w_ex = nc.dram_tensor("w_ex", [KD, 128, KD * 128], F32, kind="ExternalInput").ap()
    c_identf = nc.dram_tensor("c_identf", [128, 128], F32, kind="ExternalInput").ap()
    c_identb = nc.dram_tensor("c_identb", [128, 128], BF16, kind="ExternalInput").ap()
    c_le = nc.dram_tensor("c_le", [128, 128], F32, kind="ExternalInput").ap()
    c_leb = nc.dram_tensor("c_leb", [128, 128], BF16, kind="ExternalInput").ap()
    c_onesb = nc.dram_tensor("c_onesb", [128, 128], BF16, kind="ExternalInput").ap()
    c_ones = nc.dram_tensor("c_ones", [128, 128], F32, kind="ExternalInput").ap()
    c_sel = nc.dram_tensor("c_sel", [NCb, NCb * DS], F32, kind="ExternalInput").ap()
    c_qmask = nc.dram_tensor("c_qmask", [NCb, NCb], F32, kind="ExternalInput").ap()
    c_cmask = nc.dram_tensor("c_cmask", [NCb, 1], F32, kind="ExternalInput").ap()
    c_small = nc.dram_tensor("c_small", [NH, 4], F32, kind="ExternalInput").ap()
    c_drow = nc.dram_tensor("c_drow", [1, NH], F32, kind="ExternalInput").ap()
    c_fvals = nc.dram_tensor("c_fvals", [128, 8], F32, kind="ExternalInput").ap()
    c_nw = nc.dram_tensor("c_nw", [128, NH], F32, kind="ExternalInput").ap()
    c_bcx = nc.dram_tensor("c_bcx", [128, NH], F32, kind="ExternalInput").ap()
    c_bcbc = nc.dram_tensor("c_bcbc", [DS, 2], F32, kind="ExternalInput").ap()
    c_tpx = nc.dram_tensor("c_tpx", [128, NH * 4], F32, kind="ExternalInput").ap()
    c_tpbc = nc.dram_tensor("c_tpbc", [DS, 8], F32, kind="ExternalInput").ap()

    out = nc.dram_tensor("out", [T, D], F32, kind="ExternalOutput").ap()
    if DEV_DUMPS:
        dbg_lnu = nc.dram_tensor("dbg_lnu", [128, KD * TE], BF16, kind="ExternalOutput").ap()
        dbg_x0 = nc.dram_tensor("dbg_x0", [128, T], BF16, kind="ExternalOutput").ap()
        dbg_z0 = nc.dram_tensor("dbg_z0", [128, T], BF16, kind="ExternalOutput").ap()
        dbg_dt = nc.dram_tensor("dbg_dt", [NH, T], F32, kind="ExternalOutput").ap()
        dbg_gw0 = nc.dram_tensor("dbg_gw0", [128, T], BF16, kind="ExternalOutput").ap()
        dbg_sg2 = nc.dram_tensor("dbg_sg2", [1, T], F32, kind="ExternalOutput").ap()
        dbg_x1 = nc.dram_tensor("dbg_x1", [128, D], F32, kind="ExternalOutput").ap()
        dbg_x1n = nc.dram_tensor("dbg_x1n", [128, KD * T], BF16, kind="ExternalOutput").ap()
        dbg_h1 = nc.dram_tensor("dbg_h1", [128, KD * T], BF16, kind="ExternalOutput").ap()

    _sk = dict(kind="ExternalOutput") if DEV_DUMPS else {}
    zsil_d = nc.dram_tensor("zsil_d", [NH, 128, T], BF16, **_sk)
    xsil_d = nc.dram_tensor("xsil_d", [NH, 128, T], BF16, **_sk)
    gw_d = nc.dram_tensor("gw_d", [NH, 128, T], BF16, **_sk)
    x1_d = nc.dram_tensor("x1_d", [NTR, 128, D], F32, **_sk)
    loc_in = [nc.dram_tensor(f"loc_in{h}", [DS, 128], F32) for h in range(NH)]
    loc_out = [nc.dram_tensor(f"loc_out{h}", [NCb * DS, 128], F32) for h in range(NH)]
    lng_in = nc.dram_tensor("lng_in", [1, NH], F32)
    lng_out = nc.dram_tensor("lng_out", [NCb, NH], F32)

    xs = ExitStack()
    with xs:
        def sbuf(name, shape, dt):
            return xs.enter_context(nc.sbuf_tensor(name, shape, dt))

        identf = sbuf("identf", [128, 128], F32)
        identb = sbuf("identb", [128, 128], BF16)
        le = sbuf("le", [128, 128], F32)
        leb = sbuf("leb", [128, 128], BF16)
        onesb = sbuf("onesb", [128, 128], BF16)
        onesf = sbuf("onesf", [128, 128], F32)
        selt = sbuf("selt", [NCb, NCb * DS], F32)
        qmask = sbuf("qmask", [NCb, NCb], F32)
        cmask = sbuf("cmask", [NCb, 1], F32)
        small = sbuf("small", [NH, 4], F32)
        nwm = sbuf("nwm", [128, NH], F32)
        bcx = sbuf("bcx", [128, NH], F32)
        bcbc = sbuf("bcbc", [DS, 2], F32)
        tpx = sbuf("tpx", [128, NH * 4], F32)
        tpbc = sbuf("tpbc", [DS, 8], F32)
        aneg = sbuf("aneg", [NH, 1], F32)
        drep = sbuf("drep", [128, NH], F32)
        fvals = sbuf("fvals", [128, 8], F32)
        FV = [EPS, 1.0, -1.0, 0.0, 1.0 / DI]
        for fi, fv in enumerate(FV):
            nc.const_aps.register(F32, fv, fvals[:, fi:fi + 1])
        sg2 = sbuf("sg2", [1, T], F32)
        rr_row = sbuf("rr_row", [1, T], F32)
        lnw = sbuf("lnw", [128, 8], F32)
        lnmsq = sbuf("lnmsq", [128, 8], F32)

        sW = Ctr(xs, nc, "sW", 16)
        sIN = Ctr(xs, nc, "sIN", 16)
        sOUT = Ctr(xs, nc, "sOUT", 16)
        sT = Ctr(xs, nc, "sT")
        sV = Ctr(xs, nc, "sV")
        sS = Ctr(xs, nc, "sS")
        sG = Ctr(xs, nc, "sG", 16)
        sCC = Ctr(xs, nc, "sCC")

        block = xs.enter_context(nc.Block())
        SY, GP, TEN, VEC, SCA = block.sync, block.gpsimd, block.tensor, block.vector, block.scalar

        AX = mybir.AxisListType.X

        def emit_ln_aggr(bnb_t, bna_t, nr, wait_s=None):
            def _(e):
                if wait_s is not None:
                    e.wait_ge(sS.h, wait_s)
                means = bass.AP(bnb_t, 1, [[24, nr], [3, 8]])
                cvs = bass.AP(bnb_t, 2, [[24, nr], [3, 8]])
                e.reduce_sum(lnw[0:nr, 0:1], means, axis=AX)
                e.reduce_sum(lnw[0:nr, 1:2], cvs, axis=AX)
                sV.inc(e.tensor_mul(lnmsq[0:nr, 0:8], means, means))
                e.wait_ge(sV.h, sV.n)
                sV.inc(e.reduce_sum(lnw[0:nr, 2:3], lnmsq[0:nr, 0:8], axis=AX))
                e.wait_ge(sV.h, sV.n)
                e.tensor_scalar_mul(bna_t[0:nr, 0:1], lnw[0:nr, 0:1], 0.125)
                sV.inc(e.tensor_scalar_mul(lnw[0:nr, 3:4], lnw[0:nr, 1:2],
                                           1.0 / 2048.0))
                e.wait_ge(sV.h, sV.n)
                sV.inc(e.scalar_tensor_tensor(lnw[0:nr, 4:5], lnw[0:nr, 2:3],
                                              0.125, lnw[0:nr, 3:4],
                                              ALU.mult, ALU.add))
                e.wait_ge(sV.h, sV.n)
                sV.inc(e.tensor_mul(lnw[0:nr, 5:6], bna_t[0:nr, 0:1],
                                    bna_t[0:nr, 0:1]))
                e.wait_ge(sV.h, sV.n)
                sV.inc(e.tensor_sub(bna_t[0:nr, 1:2], lnw[0:nr, 4:5],
                                    lnw[0:nr, 5:6]))
            VEC(_)

        def _(e):
            for dst, src in [(identf, c_identf), (identb, c_identb), (le, c_le),
                             (leb, c_leb), (onesb, c_onesb), (onesf, c_ones), (selt, c_sel),
                             (qmask, c_qmask), (cmask, c_cmask), (small, c_small),
                             (nwm, c_nw), (bcx, c_bcx), (bcbc, c_bcbc),
                             (tpx, c_tpx), (tpbc, c_tpbc), (fvals, c_fvals)]:
                sIN.inc(e.dma_start(out=dst[:, :], in_=src))
        SY(_)
        n_const = sIN.n

        drow = sbuf("drow", [1, NH], F32)

        def _(e):
            sIN.inc(e.dma_start(out=drow[:, :], in_=c_drow))
        SY(_)
        n_const = sIN.n

        with nc.psum_tensor("p_p0", [128, 512], F32) as p_p0:
            def _(e):
                e.wait_ge(sIN.h, n_const)
                sS.inc(e.activation(aneg[:, 0:1], small[:, 1:2], AF.Exp))
            SCA(_)
            w1 = sS.n

            def _(e):
                e.wait_ge(sS.h, w1)
                sV.inc(e.tensor_scalar_mul(aneg[:, 0:1], aneg[:, 0:1], -1.0))
            VEC(_)

            def _(e):
                e.wait_ge(sIN.h, n_const)
                sT.inc(e.matmul(p_p0[:, 0:NH], onesf[0:1, :], drow[0:1, :],
                                start=True, stop=True))
            TEN(_)
            w4 = sT.n

            def _(e):
                e.wait_ge(sT.h, w4)
                sV.inc(e.tensor_copy(drep[:, :], p_p0[:, 0:NH]))
            VEC(_)

        with ExitStack() as ms:
            def msb(name, shape, dt):
                return ms.enter_context(nc.sbuf_tensor(name, shape, dt))

            c_fm = msb("c_fm", [DS, T], BF16)
            ac_tm = msb("ac_tm", [128, NCH * NH], F32)
            e_tm = msb("e_tm", [128, NCH * NH], F32)
            dtsp_tm = msb("dtsp_tm", [128, NCH * NH], F32)
            rcdt_tm = msb("rcdt_tm", [128, NCH * NH], F32)
            b_tm = msb("b_tm", [128, NCH * DS], BF16)
            gmt = msb("gmt", [128, NCH * 128], BF16)
            grep = msb("grep", [DS, NCH * NH], F32)
            lngr = msb("lngr", [1, NH], F32)
            coefrep = msb("coefrep", [DS, NCb * NH], F32)
            b_fm = msb("b_fm", [DS, T], BF16)
            dtsp = msb("dtsp", [NH, T], F32)
            adt = msb("adt", [NH, T], F32)

            with ExitStack() as a0:
                def a0b_(name, shape, dt):
                    return a0.enter_context(nc.sbuf_tensor(name, shape, dt))

                lnu = a0b_("lnu", [128, KD * TE], BF16)
                u_t = [a0b_("u_t0", [128, D], F32), a0b_("u_t1", [128, D], F32)]
                lnt = [a0b_("lnt0", [128, D], BF16), a0b_("lnt1", [128, D], BF16)]
                bnb = a0b_("bnb", [128, 24], F32)
                bna = a0b_("bna", [128, 2], F32)
                rstd = a0b_("rstd", [128, 1], F32)
                nmr = a0b_("nmr", [128, 1], F32)

                with nc.psum_tensor("p_tr", [128, 512], BF16) as p_tr:
                    ev_done = sV.n
                    ut_free = [[sV.n, sS.n], [sV.n, sS.n]]
                    lnt_free = [sT.n, sT.n]
                    prev_norm_s = sS.n
                    for it in range(NT):
                        r0 = it * 128
                        nr = min(128, TE - r0)
                        pb = it % 2
                        ut = u_t[pb]
                        lt = lnt[pb]

                        def _(e, r0=r0, nr=nr, ut=ut, uf=tuple(ut_free[pb])):
                            e.wait_ge(sV.h, uf[0])
                            e.wait_ge(sS.h, uf[1])
                            sIN.inc(e.dma_start(out=ut[0:nr, :], in_=u_in[r0:r0 + nr, :]))
                        SY(_)
                        w_ld = sIN.n

                        def _(e, nr=nr, w_ld=w_ld, ut=ut):
                            e.wait_ge(sIN.h, w_ld)
                            for q in range(4):
                                ins = e.bn_stats(bnb[0:nr, 6 * q:6 * q + 6],
                                                 ut[0:nr, 512 * q:512 * (q + 1)])
                            sV.inc(ins)
                            e.wait_ge(sV.h, sV.n)
                        VEC(_)
                        emit_ln_aggr(bnb, bna, nr, wait_s=prev_norm_s)
                        w_st = sV.n

                        def _(e, nr=nr, w_st=w_st):
                            e.wait_ge(sV.h, w_st)
                            sS.inc(e.activation(rstd[0:nr, :], bna[0:nr, 1:2], AF.Sqrt,
                                                bias=EPS))
                        SCA(_)
                        w_sq = sS.n

                        def _(e, nr=nr, w_sq=w_sq):
                            e.wait_ge(sS.h, w_sq)
                            sV.inc(e.reciprocal(rstd[0:nr, :], rstd[0:nr, :]))
                            e.wait_ge(sV.h, sV.n)
                            sV.inc(e.tensor_scalar(nmr[0:nr, :], bna[0:nr, 0:1],
                                                   rstd[0:nr, 0:1], -1.0,
                                                   ALU.mult, ALU.mult))
                        VEC(_)
                        w_rs = sV.n

                        def _(e, nr=nr, w_rs=w_rs, ut=ut, lt=lt, lf=lnt_free[pb]):
                            e.wait_ge(sV.h, w_rs)
                            e.wait_ge(sT.h, lf)
                            sS.inc(e.activation(lt[0:nr, :], ut[0:nr, :], AF.Identity,
                                                bias=nmr[0:nr, 0:1],
                                                scale=rstd[0:nr, 0:1]))
                        SCA(_)
                        w_no = sS.n
                        prev_norm_s = w_no
                        ut_free[pb] = [w_st, w_no]

                        for k4 in range(0, KD, 4):
                            def _(e, nr=nr, k4=k4, w_no=w_no, ev=ev_done, lt=lt):
                                e.wait_ge(sS.h, w_no)
                                e.wait_ge(sV.h, ev)
                                for kk in range(k4, k4 + 4):
                                    ins = e.transpose(
                                        p_tr[:, (kk - k4) * 128:(kk - k4) * 128 + nr],
                                        lt[0:nr, kk * 128:(kk + 1) * 128],
                                        identb[0:nr, 0:nr])
                                sT.inc(ins)
                            TEN(_)
                            w_tt = sT.n

                            def _(e, nr=nr, k4=k4, r0=r0, w_tt=w_tt):
                                e.wait_ge(sT.h, w_tt)
                                for kk in range(k4, k4 + 4):
                                    ins = e.tensor_copy(
                                        lnu[:, kk * TE + r0:kk * TE + r0 + nr],
                                        p_tr[:, (kk - k4) * 128:(kk - k4) * 128 + nr])
                                sV.inc(ins)
                            VEC(_)
                            ev_done = sV.n
                        lnt_free[pb] = sT.n

                wt = [a0b_("wt0", [128, KD * 128], BF16),
                      a0b_("wt1", [128, KD * 128], BF16)]
                xw = a0b_("xw", [128, TE], F32)
                cva = a0b_("cva", [128, T], F32)
                sil = [a0b_("sil0", [128, T], BF16), a0b_("sil1", [128, T], BF16)]

                lnu_ready = sV.n

                sched = [("dt", w_dt, NH), ("B", w_b, DS), ("C", w_c, DS)]
                sched += [(f"z{h}", w_z[h], 128) for h in range(NH)]
                sched += [(f"x{h}", w_x[h], 128) for h in range(NH)]

                def _(e):
                    sW.inc(e.dma_start(out=wt[0][:, 0:KD * NH], in_=w_dt))
                GP(_)

                with nc.psum_tensor("p_m0", [128, 512], F32) as p_m0, \
                     nc.psum_tensor("p_m1", [128, 512], F32) as p_m1:
                    pfree = [sV.n, sV.n]
                    mm_end = {}
                    sil_free = [sOUT.n, sOUT.n]
                    cva_free_s = sS.n
                    for bi, (nm, wap, rows) in enumerate(sched):
                        kind = nm[0] if nm[0] in "zx" else nm
                        h = int(nm[1:]) if kind in "zx" else 0
                        buf = wt[bi % 2]
                        w_have = sW.n
                        if bi + 1 < len(sched):
                            nm2, wap2, rows2 = sched[bi + 1]
                            nbuf = wt[(bi + 1) % 2]
                            wprev = mm_end.get(bi - 1, 0)

                            def _(e, wap2=wap2, rows2=rows2, nbuf=nbuf, wprev=wprev):
                                e.wait_ge(sT.h, wprev)
                                sW.inc(e.dma_start(out=nbuf[:, 0:KD * rows2], in_=wap2))
                            GP(_)

                        ctx_cols = kind in ("B", "C", "x")
                        cols = TE if ctx_cols else T
                        c0 = 0 if ctx_cols else CTX
                        groups = []
                        g0 = 0
                        while g0 < cols:
                            groups.append((g0, min(512, cols - g0)))
                            g0 += 512
                        ev_marks = []
                        for gi, (g0, gn) in enumerate(groups):
                            pm = [p_m0, p_m1][gi % 2]

                            def _(e, rows=rows, g0=g0, gn=gn, c0=c0, buf=buf, pm=pm,
                                  w_have=w_have, pf=pfree[gi % 2], lr=lnu_ready):
                                e.wait_ge(sW.h, w_have)
                                e.wait_ge(sV.h, max(pf, lr))
                                for kk in range(KD):
                                    ins = e.matmul(
                                        pm[0:rows, 0:gn],
                                        buf[:, kk * rows:(kk + 1) * rows],
                                        lnu[:, kk * TE + c0 + g0:kk * TE + c0 + g0 + gn],
                                        start=(kk == 0), stop=(kk == KD - 1))
                                sT.inc(ins)
                            TEN(_)
                            mm_end[bi] = sT.n
                            w_mm = sT.n

                            if kind == "dt":
                                def _(e, g0=g0, gn=gn, pm=pm, w_mm=w_mm):
                                    e.wait_ge(sT.h, w_mm)
                                    sV.inc(e.tensor_copy(dtsp[:, g0:g0 + gn],
                                                         pm[0:NH, 0:gn]))
                                VEC(_)
                            elif kind in ("B", "C"):
                                dst = xw

                                def _(e, g0=g0, gn=gn, pm=pm, w_mm=w_mm):
                                    e.wait_ge(sT.h, w_mm)
                                    sV.inc(e.tensor_copy(xw[0:DS, g0:g0 + gn],
                                                         pm[0:DS, 0:gn]))
                                VEC(_)
                            elif kind == "z":
                                def _(e, g0=g0, gn=gn, pm=pm, w_mm=w_mm, h=h,
                                      sf=sil_free[h % 2]):
                                    e.wait_ge(sT.h, w_mm)
                                    e.wait_ge(sOUT.h, sf)
                                    sS.inc(e.activation(sil[h % 2][:, g0:g0 + gn],
                                                        pm[:, 0:gn], AF.Silu))
                                SCA(_)
                            else:
                                def _(e, g0=g0, gn=gn, pm=pm, w_mm=w_mm):
                                    e.wait_ge(sT.h, w_mm)
                                    sV.inc(e.tensor_copy(xw[:, g0:g0 + gn],
                                                         pm[:, 0:gn]))
                                VEC(_)
                            if kind in ("dt", "B", "C", "x"):
                                pfree[gi % 2] = sV.n
                            else:
                                def _(e, w_s=sS.n):
                                    e.wait_ge(sS.h, w_s)
                                    sV.inc(e.engine_nop())
                                VEC(_)
                                pfree[gi % 2] = sV.n

                        ev_all = max(pfree)
                        if kind == "dt":
                            def _(e, ev=pfree[:]):
                                e.wait_ge(sV.h, max(ev))
                                sV.inc(e.tensor_scalar_add(dtsp[:, :], dtsp[:, :],
                                                           small[:, 0:1]))
                                e.wait_ge(sV.h, sV.n)
                                sV.inc(e.tensor_relu(cva[0:NH, 0:T], dtsp[:, :]))
                            VEC(_)
                            w_v = sV.n

                            def _(e, w_v=w_v):
                                e.wait_ge(sV.h, w_v)
                                e.activation(xw[0:NH, 0:T], dtsp[:, :], AF.Abs)
                                e.activation(xw[0:NH, 0:T], xw[0:NH, 0:T],
                                             AF.Exp, scale=-1.0)
                                sS.inc(e.activation(xw[0:NH, 0:T], xw[0:NH, 0:T],
                                                    AF.Ln, bias=1.0))
                            SCA(_)
                            w_s = sS.n

                            def _(e, w_s=w_s):
                                e.wait_ge(sS.h, w_s)
                                sV.inc(e.tensor_add(dtsp[:, :], xw[0:NH, 0:T],
                                                    cva[0:NH, 0:T]))
                                e.wait_ge(sV.h, sV.n)
                                sV.inc(e.tensor_scalar_mul(adt[:, :], dtsp[:, :],
                                                           aneg[:, 0:1]))
                            VEC(_)
                        elif kind in ("B", "C"):
                            ti = 0 if kind == "B" else 1

                            def _(e, ev=ev_all, ti=ti, cf=cva_free_s):
                                e.wait_ge(sV.h, ev)
                                e.wait_ge(sS.h, cf)
                                e.tensor_scalar_mul(cva[0:DS, 0:T], xw[0:DS, 0:T],
                                                    tpbc[:, 4 * ti:4 * ti + 1])
                                for j in range(1, 4):
                                    ins = e.scalar_tensor_tensor(
                                        cva[0:DS, 0:T], xw[0:DS, j:T + j],
                                        tpbc[:, 4 * ti + j:4 * ti + j + 1],
                                        cva[0:DS, 0:T], ALU.mult, ALU.add)
                                sV.inc(ins)
                            VEC(_)
                            w_cv = sV.n
                            dstt = b_fm if kind == "B" else c_fm

                            def _(e, w_cv=w_cv, dstt=dstt, ti=ti):
                                e.wait_ge(sV.h, w_cv)
                                sS.inc(e.activation(dstt[:, :], cva[0:DS, 0:T],
                                                    AF.Silu, bias=bcbc[:, ti:ti + 1]))
                            SCA(_)
                            cva_free_s = sS.n
                        elif kind == "z":
                            def _(e, h=h, w_s=sS.n):
                                e.wait_ge(sS.h, w_s)
                                sOUT.inc(e.dma_start(out=zsil_d[h], in_=sil[h % 2][:, :]))
                                if DEV_DUMPS and h == 0:
                                    sOUT.inc(e.dma_start(out=dbg_z0, in_=sil[0][:, :]))
                                    sOUT.inc(e.dma_start(out=dbg_lnu, in_=lnu[:, :]))
                                    sOUT.inc(e.dma_start(out=dbg_dt, in_=dtsp[:, :]))
                            SCA(_)
                            sil_free[h % 2] = sOUT.n
                        else:
                            def _(e, ev=ev_all, h=h, cf=cva_free_s):
                                e.wait_ge(sV.h, ev)
                                e.wait_ge(sS.h, cf)
                                e.tensor_scalar_mul(cva[:, :], xw[:, 0:T],
                                                    tpx[:, 4 * h:4 * h + 1])
                                for j in range(1, 4):
                                    ins = e.scalar_tensor_tensor(
                                        cva[:, :], xw[:, j:T + j],
                                        tpx[:, 4 * h + j:4 * h + j + 1],
                                        cva[:, :], ALU.mult, ALU.add)
                                sV.inc(ins)
                            VEC(_)
                            w_cv = sV.n

                            def _(e, w_cv=w_cv, h=h, sf=sil_free[h % 2]):
                                e.wait_ge(sV.h, w_cv)
                                e.wait_ge(sOUT.h, sf)
                                sS.inc(e.activation(sil[h % 2][:, :], cva[:, :],
                                                    AF.Silu, bias=bcx[:, h:h + 1]))
                            SCA(_)
                            cva_free_s = sS.n
                            w_si = sS.n

                            def _(e, w_si=w_si, h=h):
                                e.wait_ge(sS.h, w_si)
                                sOUT.inc(e.dma_start(out=xsil_d[h], in_=sil[h % 2][:, :]))
                                if DEV_DUMPS and h == 0:
                                    sOUT.inc(e.dma_start(out=dbg_x0, in_=sil[0][:, :]))
                            SCA(_)
                            sil_free[h % 2] = sOUT.n

            with nc.psum_tensor("p_a", [128, 512], F32) as p_a, \
                 nc.psum_tensor("p_b", [128, 512], F32) as p_b, \
                 nc.psum_tensor("p_ab", [128, 512], BF16) as p_ab, \
                 nc.sbuf_tensor("adtw", [128, NH], F32) as adtw, \
                 nc.sbuf_tensor("aclr", [NH, DS], F32) as aclr, \
                 nc.sbuf_tensor("acw", [NH, CH], F32) as acw, \
                 nc.sbuf_tensor("rcw", [NH, CH], F32) as rcw, \
                 nc.sbuf_tensor("grow", [1, NH], F32) as grow:
                pa_v = sV.n
                pa_s = sS.n
                pb_s = sS.n
                a0_spills = sOUT.n
                for c in range(NCH):
                    cs = slice(c * CH, (c + 1) * CH)
                    ch_ = slice(c * NH, (c + 1) * NH)
                    cd = slice(c * DS, (c + 1) * DS)

                    def _(e, c=c, pa=pa_v, pas=pa_s, sp=a0_spills):
                        e.wait_ge(sV.h, pa)
                        e.wait_ge(sS.h, pas)
                        e.wait_ge(sOUT.h, sp)
                        e.transpose(p_a[:, 0:NH], adt[:, c * CH:(c + 1) * CH],
                                    identf[0:NH, 0:NH])
                        sT.inc(e.transpose(p_a[:, NH:2 * NH],
                                           dtsp[:, c * CH:(c + 1) * CH],
                                           identf[0:NH, 0:NH]))
                    TEN(_)
                    w_t1 = sT.n

                    def _(e, ch_=ch_, w_t1=w_t1):
                        e.wait_ge(sT.h, w_t1)
                        e.tensor_copy(adtw[:, :], p_a[:, 0:NH])
                        sV.inc(e.tensor_copy(dtsp_tm[:, ch_], p_a[:, NH:2 * NH]))
                    VEC(_)
                    w_v1 = sV.n

                    def _(e, c=c, w_v1=w_v1, pbs=pb_s):
                        e.wait_ge(sV.h, w_v1)
                        e.wait_ge(sS.h, pbs)
                        e.matmul(p_a[0:NH, 0:CH], adtw[:, :], le[:, :],
                                 start=True, stop=True)
                        sT.inc(e.matmul(p_b[:, 0:NH], le[:, :], adtw[:, :],
                                        start=True, stop=True))
                    TEN(_)
                    w_t2 = sT.n

                    def _(e, c=c, ch_=ch_, w_t2=w_t2):
                        e.wait_ge(sT.h, w_t2)
                        sV.inc(e.tensor_copy(acw[:, :], p_a[0:NH, 0:CH]))
                        sV.inc(e.tensor_copy(ac_tm[:, ch_], p_b[:, 0:NH]))
                        e.wait_ge(sV.h, sV.n)
                    VEC(_)
                    w_v2 = sV.n

                    def _(e, ch_=ch_, w_v2=w_v2):
                        e.wait_ge(sV.h, w_v2)
                        sS.inc(e.activation(e_tm[:, ch_], p_b[:, 0:NH], AF.Exp))
                    SCA(_)
                    pb_free0 = sS.n
                    pb_s = sS.n

                    def _(e, c=c, w_v2=w_v2):
                        e.wait_ge(sV.h, w_v2)
                        sV.inc(e.tensor_scalar_mul(
                            aclr[:, :], onesf[0:NH, 0:DS],
                            acw[:, CH - 1:CH]))
                    VEC(_)
                    w_vg = sV.n

                    def _(e, w_vg=w_vg, pas=pa_s):
                        e.wait_ge(sV.h, w_vg)
                        e.wait_ge(sS.h, pas)
                        sT.inc(e.transpose(p_a[0:DS, 256:256 + NH], aclr[:, :],
                                           identf[0:NH, 0:NH]))
                    TEN(_)
                    w_t4b = sT.n

                    def _(e, c=c, w_t4b=w_t4b):
                        e.wait_ge(sT.h, w_t4b)
                        sV.inc(e.tensor_copy(grow[:, :], p_a[0:1, 256:256 + NH]))
                        e.wait_ge(sV.h, sV.n)
                        if c == 0:
                            ins = e.tensor_copy(lngr[:, :], grow[:, :])
                        else:
                            ins = e.tensor_add(lngr[:, :], lngr[:, :], grow[:, :])
                        sV.inc(ins)
                    VEC(_)

                    def _(e, c=c, w_v2=w_v2):
                        e.wait_ge(sV.h, w_v2)
                        sV.inc(e.tensor_scalar(
                            rcw[:, :], acw[:, :],
                            acw[:, CH - 1:CH], None,
                            ALU.subtract, ALU.bypass))
                    VEC(_)
                    w_v3 = sV.n

                    def _(e, w_v3=w_v3):
                        e.wait_ge(sV.h, w_v3)
                        sS.inc(e.activation(rcw[:, :], rcw[:, :], AF.Exp, scale=-1.0))
                    SCA(_)
                    w_s1 = sS.n

                    def _(e, w_s1=w_s1, pbf=pb_free0):
                        e.wait_ge(sS.h, max(w_s1, pbf))
                        sT.inc(e.transpose(p_b[:, 0:NH], rcw[:, :],
                                           identf[0:NH, 0:NH]))
                    TEN(_)
                    w_t3 = sT.n

                    def _(e, ch_=ch_, w_t3=w_t3):
                        e.wait_ge(sT.h, w_t3)
                        sV.inc(e.tensor_mul(rcdt_tm[:, ch_], p_b[:, 0:NH],
                                            dtsp_tm[:, ch_]))
                    VEC(_)

                    def _(e, ch_=ch_, w_t4b=w_t4b):
                        e.wait_ge(sT.h, w_t4b)
                        sS.inc(e.activation(grep[:, ch_], p_a[0:DS, 256:256 + NH],
                                            AF.Exp))
                    SCA(_)
                    pa_s = sS.n

                    def _(e, c=c, cs=cs, w_prev=sV.n):
                        e.wait_ge(sV.h, w_prev)
                        e.transpose(p_ab[:, 0:DS], b_fm[:, cs],
                                    identb[0:DS, 0:DS])
                        sT.inc(e.matmul(p_b[:, 128:256], b_fm[:, cs], c_fm[:, cs],
                                        start=True, stop=True))
                    TEN(_)
                    w_t5 = sT.n

                    def _(e, c=c, cs=cs, cd=cd, w_t5=w_t5):
                        e.wait_ge(sT.h, w_t5)
                        e.tensor_copy(b_tm[:, cd], p_ab[:, 0:DS])
                        sV.inc(e.tensor_mul(gmt[:, cs], p_b[:, 128:256], leb[:, :]))
                    VEC(_)
                    pa_v = sV.n

            with nc.psum_tensor("p_q", [128, 512], F32) as p_q, \
                 nc.sbuf_tensor("lgs", [NCb, NH], F32) as lgs, \
                 nc.sbuf_tensor("q_sb", [NCb, NH], F32) as q_sb, \
                 nc.sbuf_tensor("qk1", [1, NH], F32) as qk1, \
                 nc.sbuf_tensor("coef", [NCb, NH], F32) as coef:
                w_lng = sV.n

                def _(e, w_lng=w_lng):
                    e.wait_ge(sV.h, w_lng)
                    sG.inc(e.dma_start(out=lng_in.ap(), in_=lngr[:, :]))
                GP(_)
                w_g = sG.n

                def _(e, w_g=w_g):
                    e.wait_ge(sG.h, w_g)
                    sCC.inc(e.collective_compute(
                        "AllGather", ALU.bypass,
                        replica_groups=[list(range(NCb))],
                        ins=[lng_in.ap().opt()], outs=[lng_out.ap().opt()]))
                GP(_)
                w_cc = sCC.n

                def _(e, w_cc=w_cc):
                    e.wait_ge(sCC.h, w_cc)
                    sIN.inc(e.dma_start(out=lgs[:, :], in_=lng_out.ap()))
                SY(_)
                w_lgs = sIN.n

                def _(e, w_lgs=w_lgs):
                    e.wait_ge(sIN.h, w_lgs)
                    e.matmul(p_q[0:NCb, 0:NH], le[0:NCb, 0:NCb], lgs[:, :],
                             start=True, stop=True)
                    sT.inc(e.matmul(p_q[0:NCb, 2 * NH:3 * NH], qmask[:, :],
                                    lgs[:, :], start=True, stop=True))
                TEN(_)
                w_tq = sT.n

                def _(e, w_tq=w_tq):
                    e.wait_ge(sT.h, w_tq)
                    sV.inc(e.tensor_copy(q_sb[:, :], p_q[0:NCb, 0:NH]))
                    e.wait_ge(sV.h, sV.n)
                VEC(_)
                w_vq = sV.n

                def _(e, w_vq=w_vq):
                    e.wait_ge(sV.h, w_vq)
                    sV.inc(e.tensor_sub(coef[:, :], p_q[0:NCb, 2 * NH:3 * NH],
                                        q_sb[:, :]))
                    e.wait_ge(sV.h, sV.n)
                    sV.inc(e.tensor_scalar_min(coef[:, :], coef[:, :], 0.0))
                VEC(_)
                w_vc = sV.n

                def _(e, w_vc=w_vc):
                    e.wait_ge(sV.h, w_vc)
                    sS.inc(e.activation(coef[:, :], coef[:, :], AF.Exp))
                SCA(_)
                w_sc = sS.n

                def _(e, w_sc=w_sc):
                    e.wait_ge(sS.h, w_sc)
                    sV.inc(e.tensor_scalar_mul(coef[:, :], coef[:, :], cmask[:, 0:1]))
                VEC(_)
                w_vm = sV.n

                for kp in range(NCb):
                    def _(e, kp=kp, w_vm=w_vm, pvf=sV.n):
                        e.wait_ge(sV.h, max(w_vm, pvf))
                        sT.inc(e.matmul(p_q[0:DS, 3 * NH + 0:3 * NH + NH],
                                        selt[:, kp * DS:(kp + 1) * DS], coef[:, :],
                                        start=True, stop=True))
                    TEN(_)
                    w_tk = sT.n

                    def _(e, kp=kp, w_tk=w_tk):
                        e.wait_ge(sT.h, w_tk)
                        sV.inc(e.tensor_copy(coefrep[:, kp * NH:(kp + 1) * NH],
                                             p_q[0:DS, 3 * NH:4 * NH]))
                    VEC(_)

            with ExitStack() as hl:
                def hsb(name, shape, dt):
                    return hl.enter_context(nc.sbuf_tensor(name, shape, dt))

                xsb = hsb("xsb", [128, T], BF16)
                zsb = hsb("zsb", [128, T], BF16)
                x_tm = hsb("x_tm", [128, NCH * 128], BF16)
                sraw = hsb("sraw", [DS, NCH * 128], F32)
                bw_t = hsb("bw_t", [128, DS], BF16)
                loc_t = hsb("loc_t", [DS, 128], F32)
                bk = [hsb("bk0", [DS, 128], F32), hsb("bk1", [DS, 128], F32)]
                sinb = hsb("sinb", [DS, 128], BF16)
                gath = hsb("gath", [DS, NCb * 128], F32)
                crep = hsb("crep", [128, 128], F32)
                rr_t = hsb("rr_t", [128, 128], F32)
                texp = hsb("texp", [128, 128], BF16)
                mt_t = hsb("mt_t", [128, 128], BF16)
                yt1 = hsb("yt1", [128, 128], F32)
                y_tm = hsb("y_tm", [128, 128], F32)
                t_fm = hsb("t_fm", [128, 128], F32)
                g_all = hsb("g_all", [128, T], BF16)
                g2b = hsb("g2b", [128, T], BF16)
                gwb = hsb("gwb", [128, T], BF16)

                with nc.psum_tensor("p_x", [128, 512], BF16) as p_x, \
                     nc.psum_tensor("p_s", [128, 512], F32) as p_s, \
                     nc.psum_tensor("p_rt", [128, 512], F32) as p_rt, \
                     nc.psum_tensor("p_y", [128, 512], F32) as p_y, \
                     nc.psum_tensor("p_yo", [128, 512], F32) as p_yo, \
                     nc.psum_tensor("p_yf", [128, 512], F32) as p_yf, \
                     nc.psum_tensor("p_sg", [128, 512], F32) as p_sg:

                    px_free = sV.n
                    ps_free = sV.n
                    prt_free = sS.n
                    py_free = sV.n
                    pyo_free = sV.n
                    pyf_free = sV.n
                    psg_free = sV.n
                    xz_free = [sV.n, sS.n]
                    spill_prev = sOUT.n

                    for h in range(NH):
                        def _(e, h=h, xz=tuple(xz_free), sp=spill_prev):
                            e.wait_ge(sV.h, xz[0])
                            e.wait_ge(sS.h, xz[1])
                            e.wait_ge(sOUT.h, sp)
                            sIN.inc(e.dma_start(out=xsb[:, :], in_=xsil_d[h]))
                            sIN.inc(e.dma_start(out=zsb[:, :], in_=zsil_d[h]))
                        SY(_)
                        w_xz = sIN.n

                        loc_mark = 0
                        for c in range(NCH):
                            cs = slice(c * CH, (c + 1) * CH)
                            col = c * NH + h
                            cd = slice(c * DS, (c + 1) * DS)

                            def _(e, c=c, cs=cs, w_xz=w_xz, pxf=px_free):
                                e.wait_ge(sIN.h, w_xz)
                                e.wait_ge(sV.h, pxf)
                                sT.inc(e.transpose(p_x[:, 0:128], xsb[:, cs],
                                                   identb[:, :]))
                            TEN(_)
                            w_tx = sT.n

                            def _(e, c=c, cd=cd, col=col, w_tx=w_tx):
                                e.wait_ge(sT.h, w_tx)
                                e.tensor_copy(x_tm[:, c * 128:(c + 1) * 128],
                                              p_x[:, 0:128])
                                sV.inc(e.tensor_scalar_mul(
                                    bw_t[:, :], b_tm[:, cd],
                                    rcdt_tm[:, col:col + 1]))
                            VEC(_)
                            px_free = sV.n
                            w_vx = sV.n

                            def _(e, c=c, w_vx=w_vx, psf=ps_free):
                                e.wait_ge(sV.h, max(w_vx, psf))
                                sT.inc(e.matmul(p_s[0:DS, 0:128], bw_t[:, :],
                                                x_tm[:, c * 128:(c + 1) * 128],
                                                start=True, stop=True))
                            TEN(_)
                            w_ts = sT.n

                            def _(e, c=c, col=col, w_ts=w_ts):
                                e.wait_ge(sT.h, w_ts)
                                ins = e.tensor_copy(sraw[:, c * 128:(c + 1) * 128],
                                                    p_s[0:DS, 0:128])
                                if c == 0:
                                    ins = e.tensor_copy(loc_t[:, :],
                                                        sraw[:, 0:128])
                                else:
                                    ins = e.scalar_tensor_tensor(
                                        loc_t[:, :], loc_t[:, :],
                                        grep[:, col:col + 1],
                                        sraw[:, c * 128:(c + 1) * 128],
                                        ALU.mult, ALU.add)
                                sV.inc(ins)
                            VEC(_)
                            ps_free = sV.n
                            loc_mark = sV.n

                        def _(e, h=h, lm=loc_mark):
                            e.wait_ge(sV.h, lm)
                            sG.inc(e.dma_start(out=loc_in[h].ap(), in_=loc_t[:, :]))
                        GP(_)
                        w_gl = sG.n

                        def _(e, h=h, w_gl=w_gl):
                            e.wait_ge(sG.h, w_gl)
                            sCC.inc(e.collective_compute(
                                "AllGather", ALU.bypass,
                                replica_groups=[list(range(NCb))],
                                ins=[loc_in[h].ap().opt()],
                                outs=[loc_out[h].ap().opt()]))
                        GP(_)
                        w_cch = sCC.n

                        def _(e, h=h, w_cch=w_cch):
                            e.wait_ge(sCC.h, w_cch)
                            for kp in range(NCb):
                                sIN.inc(e.dma_start(
                                    out=gath[:, kp * 128:(kp + 1) * 128],
                                    in_=loc_out[h].ap()[kp * DS:(kp + 1) * DS, :]))
                        SY(_)
                        w_ga = sIN.n

                        def _(e, h=h, w_ga=w_ga):
                            e.wait_ge(sIN.h, w_ga)
                            ins = e.tensor_scalar_mul(bk[0][:, :], gath[:, 0:128],
                                                      coefrep[:, h:h + 1])
                            for kp in range(1, NCb):
                                ins = e.scalar_tensor_tensor(
                                    bk[kp % 2][:, :],
                                    gath[:, kp * 128:(kp + 1) * 128],
                                    coefrep[:, kp * NH + h:kp * NH + h + 1],
                                    bk[(kp + 1) % 2][:, :], ALU.mult, ALU.add)
                            sV.inc(ins)
                        VEC(_)
                        sabs = bk[(NCb - 1) % 2]
                        sabs_o = bk[NCb % 2]
                        w_bk = sV.n

                        for c in range(NCH):
                            cs = slice(c * CH, (c + 1) * CH)
                            col = c * NH + h

                            def _(e, w_bk=sV.n, sabs=sabs):
                                e.wait_ge(sV.h, w_bk)
                                sV.inc(e.tensor_copy(sinb[:, :], sabs[:, :]))
                            VEC(_)
                            w_si = sV.n

                            def _(e, c=c, col=col, sabs=sabs, sabs_o=sabs_o,
                                  w_si=w_si):
                                e.wait_ge(sV.h, w_si)
                                sV.inc(e.scalar_tensor_tensor(
                                    sabs_o[:, :], sabs[:, :], grep[:, col:col + 1],
                                    sraw[:, c * 128:(c + 1) * 128],
                                    ALU.mult, ALU.add))
                            VEC(_)
                            sabs, sabs_o = sabs_o, sabs

                            def _(e, col=col, prt=prt_free, w_si=w_si):
                                e.wait_ge(sS.h, prt)
                                e.wait_ge(sV.h, w_si)
                                sS.inc(e.activation(crep[:, :], onesf[:, :], AF.Identity,
                                                    scale=ac_tm[:, col:col + 1]))
                            SCA(_)
                            w_cr = sS.n

                            def _(e, w_cr=w_cr):
                                e.wait_ge(sS.h, w_cr)
                                sT.inc(e.transpose(p_rt[:, 0:128], crep[:, :],
                                                   identf[:, :]))
                            TEN(_)
                            w_rt = sT.n

                            def _(e, col=col, w_rt=w_rt):
                                e.wait_ge(sT.h, w_rt)
                                e.activation(rr_t[:, :], p_rt[:, 0:128], AF.Relu,
                                             bias=ac_tm[:, col:col + 1], scale=-1.0)
                                sS.inc(e.activation(texp[:, :], rr_t[:, :], AF.Exp,
                                                    scale=-1.0))
                            SCA(_)
                            prt_free = sS.n
                            w_te = sS.n

                            def _(e, c=c, cs=cs, col=col, w_te=w_te):
                                e.wait_ge(sS.h, w_te)
                                sV.inc(e.scalar_tensor_tensor(
                                    mt_t[:, :], texp[:, :],
                                    dtsp_tm[:, col:col + 1], gmt[:, cs],
                                    ALU.mult, ALU.mult))
                            VEC(_)
                            w_mt = sV.n

                            def _(e, c=c, cs=cs, w_mt=w_mt, pyf_=py_free,
                                  pyof=pyo_free, w_si=w_si):
                                e.wait_ge(sV.h, max(w_mt, pyf_, pyof))
                                e.matmul(p_y[:, 0:128], mt_t[:, :],
                                         x_tm[:, c * 128:(c + 1) * 128],
                                         start=True, stop=True)
                                sT.inc(e.matmul(p_yo[:, 0:128], c_fm[:, cs],
                                                sinb[:, :], start=True, stop=True))
                            TEN(_)
                            w_ty = sT.n

                            def _(e, col=col, w_ty=w_ty):
                                e.wait_ge(sT.h, w_ty)
                                e.tensor_scalar_mul(yt1[:, :], p_yo[:, 0:128],
                                                    e_tm[:, col:col + 1])
                                sV.inc(e.tensor_add(y_tm[:, :], yt1[:, :],
                                                    p_y[:, 0:128]))
                            VEC(_)
                            py_free = sV.n
                            pyo_free = sV.n
                            w_vy = sV.n

                            def _(e, w_vy=w_vy, pyff=pyf_free):
                                e.wait_ge(sV.h, max(w_vy, pyff))
                                sT.inc(e.transpose(p_yf[:, 0:128], y_tm[:, :],
                                                   identf[:, :]))
                            TEN(_)
                            w_tf = sT.n

                            def _(e, c=c, cs=cs, h=h, w_tf=w_tf):
                                e.wait_ge(sT.h, w_tf)
                                e.scalar_tensor_tensor(
                                    t_fm[:, :], xsb[:, cs], drep[:, h:h + 1],
                                    p_yf[:, 0:128], ALU.mult, ALU.add)
                                sV.inc(e.tensor_mul(g_all[:, cs], t_fm[:, :],
                                                    zsb[:, cs]))
                            VEC(_)
                            pyf_free = sV.n

                        w_g = sV.n

                        def _(e, w_g=w_g):
                            e.wait_ge(sV.h, w_g)
                            sS.inc(e.activation(g2b[:, :], g_all[:, :], AF.Square))
                        SCA(_)
                        w_g2 = sS.n

                        for gi in range(n512):
                            def _(e, gi=gi, w_g2=w_g2, psgf=psg_free):
                                e.wait_ge(sS.h, w_g2)
                                e.wait_ge(sV.h, psgf)
                                sT.inc(e.matmul(p_sg[:, 0:512], onesb[:, :],
                                                g2b[:, gi * 512:(gi + 1) * 512],
                                                start=True, stop=True))
                            TEN(_)
                            w_tg = sT.n

                            def _(e, gi=gi, h=h, w_tg=w_tg):
                                e.wait_ge(sT.h, w_tg)
                                if h == 0:
                                    ins = e.tensor_copy(
                                        sg2[:, gi * 512:(gi + 1) * 512],
                                        p_sg[0:1, 0:512])
                                else:
                                    ins = e.tensor_add(
                                        sg2[:, gi * 512:(gi + 1) * 512],
                                        sg2[:, gi * 512:(gi + 1) * 512],
                                        p_sg[0:1, 0:512])
                                sV.inc(ins)
                            VEC(_)
                            psg_free = sV.n

                        def _(e, h=h, w_g=w_g, sp=spill_prev):
                            e.wait_ge(sV.h, w_g)
                            e.wait_ge(sOUT.h, sp)
                            sV.inc(e.tensor_scalar_mul(gwb[:, :], g_all[:, :],
                                                       nwm[:, h:h + 1]))
                        VEC(_)
                        w_gw = sV.n

                        def _(e, h=h, w_gw=w_gw):
                            e.wait_ge(sV.h, w_gw)
                            sOUT.inc(e.dma_start(out=gw_d[h], in_=gwb[:, :]))
                            if DEV_DUMPS and h == 0:
                                sOUT.inc(e.dma_start(out=dbg_gw0, in_=gwb[:, :]))
                        SCA(_)
                        spill_prev = sOUT.n
                        xz_free = [sV.n, sS.n]

        with nc.psum_tensor("p_rr", [128, 512], F32) as p_rr:
            w_sg = sV.n

            def _(e, w_sg=w_sg):
                e.wait_ge(sV.h, w_sg)
                sV.inc(e.tensor_scalar(rr_row[:, :], sg2[:, :], 1.0 / DI, EPS,
                                       ALU.mult, ALU.add))
            VEC(_)
            w_r1 = sV.n

            def _(e, w_r1=w_r1):
                e.wait_ge(sV.h, w_r1)
                sS.inc(e.activation(rr_row[:, :], rr_row[:, :], AF.Sqrt))
            SCA(_)
            w_r2 = sS.n

            def _(e, w_r2=w_r2):
                e.wait_ge(sS.h, w_r2)
                sV.inc(e.reciprocal(rr_row[:, :], rr_row[:, :]))
            VEC(_)
            w_r3 = sV.n
            if DEV_DUMPS:
                def _(e, w_sg2=sV.n):
                    e.wait_ge(sV.h, w_sg2)
                    sOUT.inc(e.dma_start(out=dbg_sg2, in_=sg2[:, :]))
                SCA(_)

        x1n_fm = sbuf("x1n_fm", [128, KD * T], BF16)

        with ExitStack() as ws:
            def wsb(name, shape, dt):
                return ws.enter_context(nc.sbuf_tensor(name, shape, dt))

            rrb = wsb("rrb", [128, T], F32)
            gws = wsb("gws", [128, 32 * 512], BF16)
            wob = [wsb("wob0", [128, 32 * 128], BF16),
                   wsb("wob1", [128, 32 * 128], BF16)]
            aw = wsb("aw", [128, 512], F32)
            u_ts = [wsb(f"u_ts{q}", [128, D], F32) for q in range(4)]
            bnb2 = wsb("bnb2", [128, 24], F32)
            bna2 = wsb("bna2", [128, 2], F32)
            rstd2 = wsb("rstd2", [128, 1], F32)
            nmr2 = wsb("nmr2", [128, 1], F32)
            lnt2 = [wsb("lnt2a", [128, D], BF16), wsb("lnt2b", [128, D], BF16)]

            with nc.psum_tensor("p_w", [128, 512], F32) as p_w, \
                 nc.psum_tensor("p_t2", [128, 512], F32) as p_t2, \
                 nc.psum_tensor("p_t2b", [128, 512], BF16) as p_t2b:
                for gi in range(n512):
                    def _(e, gi=gi, w_r3=w_r3, pwf=sV.n):
                        e.wait_ge(sV.h, max(w_r3, pwf))
                        sT.inc(e.matmul(p_w[:, 0:512], onesf[0:1, :],
                                        rr_row[0:1, gi * 512:(gi + 1) * 512],
                                        start=True, stop=True))
                    TEN(_)
                    w_tb = sT.n

                    def _(e, gi=gi, w_tb=w_tb):
                        e.wait_ge(sT.h, w_tb)
                        sV.inc(e.tensor_copy(rrb[:, gi * 512:(gi + 1) * 512],
                                             p_w[:, 0:512]))
                    VEC(_)
                rr_done = sV.n

                def _(e, bt=sT.n, bo=sOUT.n):
                    e.wait_ge(sT.h, bt)
                    e.wait_ge(sOUT.h, bo)
                    sW.inc(e.dma_start(out=wob[0][:, :], in_=w_out[0]))
                GP(_)

                pw_free = rr_done
                pt2_free = sV.n
                wo_mm_end = {}
                prev_tg_mm = sT.n
                lnt2_free = [sT.n, sT.n]
                prev_norm2_s = sS.n
                for tgi in range(n512):
                    g0 = tgi * 512
                    def _(e, g0=g0, sp=sOUT.n, pm=prev_tg_mm):
                        e.wait_ge(sOUT.h, sp)
                        e.wait_ge(sT.h, pm)
                        for kk in range(32):
                            sIN.inc(e.dma_start(out=gws[:, kk * 512:(kk + 1) * 512],
                                                in_=gw_d[kk][:, g0:g0 + 512]))
                        for q in range(4):
                            r0 = CTX + g0 + q * 128
                            sIN.inc(e.dma_start(out=u_ts[q][:, :],
                                                in_=u_in[r0:r0 + 128, :]))
                    SY(_)
                    w_gws = sIN.n

                    for o in range(KD):
                        bi = tgi * KD + o
                        buf = wob[bi % 2]
                        w_wo = sW.n
                        if bi + 1 < n512 * KD:
                            o2 = (bi + 1) % KD
                            nbuf = wob[(bi + 1) % 2]

                            def _(e, o2=o2, nbuf=nbuf, wprev=wo_mm_end.get(bi - 1, 0)):
                                e.wait_ge(sT.h, wprev)
                                sW.inc(e.dma_start(out=nbuf[:, :], in_=w_out[o2]))
                            GP(_)

                        def _(e, o=o, buf=buf, w_wo=w_wo, w_gws=w_gws, pwf=pw_free):
                            e.wait_ge(sW.h, w_wo)
                            e.wait_ge(sIN.h, w_gws)
                            e.wait_ge(sV.h, pwf)
                            for kk in range(32):
                                ins = e.matmul(p_w[:, 0:512],
                                               buf[:, kk * 128:(kk + 1) * 128],
                                               gws[:, kk * 512:(kk + 1) * 512],
                                               start=(kk == 0), stop=(kk == 31))
                            sT.inc(ins)
                        TEN(_)
                        wo_mm_end[bi] = sT.n
                        w_mm = sT.n

                        def _(e, g0=g0, w_mm=w_mm):
                            e.wait_ge(sT.h, w_mm)
                            sV.inc(e.tensor_mul(aw[:, :], p_w[:, 0:512],
                                                rrb[:, g0:g0 + 512]))
                        VEC(_)
                        pw_free = sV.n
                        w_aw = sV.n

                        for q in range(4):
                            def _(e, q=q, w_aw=w_aw, ptf=pt2_free):
                                e.wait_ge(sV.h, max(w_aw, ptf))
                                sT.inc(e.transpose(p_t2[:, 0:128],
                                                   aw[:, q * 128:(q + 1) * 128],
                                                   identf[:, :]))
                            TEN(_)
                            w_tt2 = sT.n

                            def _(e, q=q, o=o, w_tt2=w_tt2):
                                e.wait_ge(sT.h, w_tt2)
                                sV.inc(e.tensor_add(
                                    u_ts[q][:, o * 128:(o + 1) * 128],
                                    u_ts[q][:, o * 128:(o + 1) * 128],
                                    p_t2[:, 0:128]))
                            VEC(_)
                            pt2_free = sV.n

                    x1_done = sV.n
                    for q in range(4):
                        it = tgi * 4 + q

                        def _(e, q=q, w_x1=x1_done):
                            e.wait_ge(sV.h, w_x1)
                            for p4 in range(4):
                                ins = e.bn_stats(bnb2[:, 6 * p4:6 * p4 + 6],
                                                 u_ts[q][:, 512 * p4:512 * (p4 + 1)])
                            sV.inc(ins)
                            e.wait_ge(sV.h, sV.n)
                        VEC(_)
                        emit_ln_aggr(bnb2, bna2, 128, wait_s=prev_norm2_s)
                        w_st2 = sV.n

                        def _(e, w_st2=w_st2):
                            e.wait_ge(sV.h, w_st2)
                            sS.inc(e.activation(rstd2[:, :], bna2[:, 1:2], AF.Sqrt,
                                                bias=EPS))
                        SCA(_)
                        w_sq2 = sS.n

                        def _(e, w_sq2=w_sq2):
                            e.wait_ge(sS.h, w_sq2)
                            sV.inc(e.reciprocal(rstd2[:, :], rstd2[:, :]))
                            e.wait_ge(sV.h, sV.n)
                            sV.inc(e.tensor_scalar(nmr2[:, :], bna2[:, 0:1],
                                                   rstd2[:, 0:1], -1.0,
                                                   ALU.mult, ALU.mult))
                        VEC(_)
                        w_rs2 = sV.n

                        def _(e, q=q, w_rs2=w_rs2, lf=lnt2_free[q % 2]):
                            e.wait_ge(sV.h, w_rs2)
                            e.wait_ge(sT.h, lf)
                            sS.inc(e.activation(lnt2[q % 2][:, :], u_ts[q][:, :],
                                                AF.Identity, bias=nmr2[:, 0:1],
                                                scale=rstd2[:, 0:1]))
                        SCA(_)
                        w_no2 = sS.n
                        prev_norm2_s = w_no2

                        def _(e, q=q, it=it, w_no2=w_no2):
                            e.wait_ge(sS.h, w_no2)
                            sOUT.inc(e.dma_start(out=x1_d[it], in_=u_ts[q][:, :]))
                            if DEV_DUMPS and it == 0:
                                sOUT.inc(e.dma_start(out=dbg_x1, in_=u_ts[0][:, :]))
                        SCA(_)

                        for k4 in range(0, KD, 4):
                            def _(e, k4=k4, w_no2=w_no2, ptf=pt2_free):
                                e.wait_ge(sS.h, w_no2)
                                e.wait_ge(sV.h, ptf)
                                for kk in range(k4, k4 + 4):
                                    ins = e.transpose(
                                        p_t2b[:, (kk - k4) * 128:(kk - k4 + 1) * 128],
                                        lnt2[q % 2][:, kk * 128:(kk + 1) * 128],
                                        identb[:, :])
                                sT.inc(ins)
                            TEN(_)
                            w_tt3 = sT.n

                            def _(e, k4=k4, it=it, w_tt3=w_tt3):
                                e.wait_ge(sT.h, w_tt3)
                                for kk in range(k4, k4 + 4):
                                    ins = e.tensor_copy(
                                        x1n_fm[:, kk * T + it * 128:
                                               kk * T + (it + 1) * 128],
                                        p_t2b[:, (kk - k4) * 128:(kk - k4) * 128 + 128])
                                sV.inc(ins)
                            VEC(_)
                            pt2_free = sV.n
                        lnt2_free[q % 2] = sT.n
                    prev_tg_mm = sT.n

        with ExitStack() as fs:
            def fsb(name, shape, dt):
                return fs.enter_context(nc.sbuf_tensor(name, shape, dt))

            h1_fm = fsb("h1_fm", [128, KD * T], BF16)
            wfb = [fsb("wfb0", [128, KD * 128], BF16),
                   fsb("wfb1", [128, KD * 128], BF16)]
            web = [fsb("web0", [128, KD * 128], BF16),
                   fsb("web1", [128, KD * 128], BF16)]
            ff = fsb("ff", [128, 512], F32)
            x1r = [fsb(f"x1r{i}", [128, 128], F32) for i in range(4)]
            ot = [fsb("ot0", [128, 128], F32), fsb("ot1", [128, 128], F32)]

            with nc.psum_tensor("p_f", [128, 512], F32) as p_f, \
                 nc.psum_tensor("p_t3", [128, 512], F32) as p_t3:
                def _(e, bt=sT.n, bv=sV.n, bs=sS.n, bo=sOUT.n):
                    e.wait_ge(sT.h, bt)
                    e.wait_ge(sV.h, bv)
                    e.wait_ge(sS.h, bs)
                    e.wait_ge(sOUT.h, bo)
                    sW.inc(e.dma_start(out=wfb[0][:, :], in_=w_up[0]))
                GP(_)
                x1n_ready = sV.n
                pf_free = sS.n
                f1_mm_end = {}
                for o in range(KD):
                    buf = wfb[o % 2]
                    w_wf = sW.n
                    if o + 1 < KD:
                        def _(e, o=o, wprev=f1_mm_end.get(o - 1, 0)):
                            e.wait_ge(sT.h, wprev)
                            sW.inc(e.dma_start(out=wfb[(o + 1) % 2][:, :],
                                               in_=w_up[o + 1]))
                        GP(_)
                    for tgi in range(n512):
                        g0 = tgi * 512

                        def _(e, o=o, g0=g0, buf=buf, w_wf=w_wf, pff=pf_free,
                              xr=x1n_ready):
                            e.wait_ge(sW.h, w_wf)
                            e.wait_ge(sS.h, pff)
                            e.wait_ge(sV.h, xr)
                            for kk in range(KD):
                                ins = e.matmul(p_f[:, 0:512],
                                               buf[:, kk * 128:(kk + 1) * 128],
                                               x1n_fm[:, kk * T + g0:kk * T + g0 + 512],
                                               start=(kk == 0), stop=(kk == KD - 1))
                            sT.inc(ins)
                        TEN(_)
                        f1_mm_end[o] = sT.n
                        w_f1 = sT.n

                        def _(e, o=o, g0=g0, w_f1=w_f1):
                            e.wait_ge(sT.h, w_f1)
                            sS.inc(e.activation(h1_fm[:, o * T + g0:o * T + g0 + 512],
                                                p_f[:, 0:512], AF.Relu))
                        SCA(_)
                        pf_free = sS.n

                h1_ready = sS.n

                def _(e, hr=h1_ready):
                    e.wait_ge(sS.h, hr)
                    sW.inc(e.dma_start(out=web[0][:, :], in_=w_ex[0]))
                    sW.inc(e.dma_start(out=wfb[0][:, :], in_=w_dn[0]))
                GP(_)
                f2_mm_end = {}
                pf2_free = sV.n
                pt3_free = sV.n
                x1_ready = sOUT.n
                x1r_free = [sV.n] * 4
                ot_free = [sOUT.n, sOUT.n]
                for dp in range(KD):
                    w_wf2 = sW.n
                    ebuf = web[dp % 2]
                    dbuf = wfb[dp % 2]
                    if dp + 1 < KD:
                        def _(e, dp=dp, wprev=f2_mm_end.get(dp - 1, 0)):
                            e.wait_ge(sT.h, wprev)
                            sW.inc(e.dma_start(out=web[(dp + 1) % 2][:, :],
                                               in_=w_ex[dp + 1]))
                            sW.inc(e.dma_start(out=wfb[(dp + 1) % 2][:, :],
                                               in_=w_dn[dp + 1]))
                        GP(_)
                    for tgi in range(n512):
                        g0 = tgi * 512

                        def _(e, dp=dp, tgi=tgi, xr=x1_ready, xf=tuple(x1r_free)):
                            e.wait_ge(sOUT.h, xr)
                            for q in range(4):
                                e.wait_ge(sV.h, xf[q])
                                it = tgi * 4 + q
                                sIN.inc(e.dma_start(
                                    out=x1r[q][:, :],
                                    in_=x1_d[it][:, dp * 128:(dp + 1) * 128]))
                        SY(_)
                        w_x1r_all = sIN.n

                        def _(e, dp=dp, g0=g0, ebuf=ebuf, dbuf=dbuf, w_wf2=w_wf2,
                              pff=pf2_free, hr=h1_ready):
                            e.wait_ge(sW.h, w_wf2)
                            e.wait_ge(sS.h, hr)
                            e.wait_ge(sV.h, pff)
                            for kk in range(KD):
                                e.matmul(p_f[:, 0:512],
                                         ebuf[:, kk * 128:(kk + 1) * 128],
                                         x1n_fm[:, kk * T + g0:kk * T + g0 + 512],
                                         start=(kk == 0), stop=False)
                            for kk in range(KD):
                                ins = e.matmul(p_f[:, 0:512],
                                               dbuf[:, kk * 128:(kk + 1) * 128],
                                               h1_fm[:, kk * T + g0:kk * T + g0 + 512],
                                               start=False, stop=(kk == KD - 1))
                            sT.inc(ins)
                        TEN(_)
                        f2_mm_end[dp] = sT.n
                        w_f2 = sT.n

                        def _(e, w_f2=w_f2):
                            e.wait_ge(sT.h, w_f2)
                            sV.inc(e.tensor_copy(ff[:, :], p_f[:, 0:512]))
                        VEC(_)
                        pf2_free = sV.n
                        w_ff = sV.n

                        for q in range(4):
                            it = tgi * 4 + q
                            qb = q % 2

                            def _(e, q=q, w_ff=w_ff, ptf=pt3_free):
                                e.wait_ge(sV.h, max(w_ff, ptf))
                                sT.inc(e.transpose(p_t3[:, 0:128],
                                                   ff[:, q * 128:(q + 1) * 128],
                                                   identf[:, :]))
                            TEN(_)
                            w_tr3 = sT.n

                            def _(e, q=q, qb=qb, w_tr3=w_tr3, w_x1r=w_x1r_all,
                                  of=ot_free[qb]):
                                e.wait_ge(sT.h, w_tr3)
                                e.wait_ge(sIN.h, w_x1r)
                                e.wait_ge(sOUT.h, of)
                                sV.inc(e.tensor_add(ot[qb][:, :], x1r[q][:, :],
                                                    p_t3[:, 0:128]))
                            VEC(_)
                            pt3_free = sV.n
                            x1r_free[q] = sV.n
                            w_ot = sV.n

                            def _(e, it=it, dp=dp, qb=qb, w_ot=w_ot):
                                e.wait_ge(sV.h, w_ot)
                                sOUT.inc(e.dma_start(
                                    out=out[it * 128:(it + 1) * 128,
                                            dp * 128:(dp + 1) * 128],
                                    in_=ot[qb][:, :]))
                            SCA(_)
                            ot_free[qb] = sOUT.n

            if DEV_DUMPS:
                def _(e, w=sT.n):
                    e.wait_ge(sT.h, w)
                    sOUT.inc(e.dma_start(out=dbg_x1n, in_=x1n_fm[:, :]))
                    sOUT.inc(e.dma_start(out=dbg_h1, in_=h1_fm[:, :]))
                SCA(_)

        def _(e, fin=sOUT.n):
            e.wait_ge(sOUT.h, fin)
        GP(_)

    return nc


def _prep_inputs(inputs, T, n_cores):
    u = np.asarray(inputs["L0_attn_u"])[0]
    W_in = np.asarray(inputs["L0_attn_W_in"])
    W_conv = np.asarray(inputs["L0_attn_W_conv"])
    b_conv = np.asarray(inputs["L0_attn_b_conv"])
    dt_bias = np.asarray(inputs["L0_attn_dt_bias"])
    A_log = np.asarray(inputs["L0_attn_A_log"])
    D_sk = np.asarray(inputs["L0_attn_D"])
    norm_w = np.asarray(inputs["L0_attn_norm_w"])
    W_out = np.asarray(inputs["L0_attn_W_out"])
    W_up = np.asarray(inputs["L0_ffn_W_up"])
    W_dn = np.asarray(inputs["L0_ffn_W_down"])
    W_ex = np.asarray(inputs["L0_ffn_W_expert"])

    L = u.shape[0]
    assert L == T * n_cores

    def blk_lhsT(W, R):
        nb = W.shape[0] // R
        a = W.reshape(nb, R, W.shape[1] // 128, 128)
        return np.ascontiguousarray(a.transpose(0, 3, 2, 1)).reshape(
            nb, 128, -1)

    wz = blk_lhsT(W_in[0:DI], 128)
    wxx = blk_lhsT(W_in[DI:2 * DI], 128)
    wb = blk_lhsT(W_in[2 * DI:2 * DI + DS], DS)[0]
    wc = blk_lhsT(W_in[2 * DI + DS:2 * DI + 2 * DS], DS)[0]
    wdt = blk_lhsT(W_in[2 * DI + 2 * DS:], NH)[0]
    wo = blk_lhsT(W_out, 128)
    wu = blk_lhsT(W_up, 128)
    wd = blk_lhsT(W_dn, 128)
    we = np.ascontiguousarray(
        W_ex.reshape(KD, 128, KD, 128).transpose(2, 1, 0, 3)).reshape(KD, 128, -1)

    jj, ii = np.meshgrid(np.arange(128), np.arange(128), indexing="ij")
    le = (jj <= ii).astype(np.float32)
    ident = np.eye(128, dtype=np.float32)
    sel = np.zeros((n_cores, n_cores * DS), np.float32)
    for kp in range(n_cores):
        sel[kp, kp * DS:(kp + 1) * DS] = 1.0
    small = np.stack([dt_bias, A_log, D_sk, np.zeros(NH, np.float32)],
                     axis=1).astype(np.float32)
    nwm = norm_w.reshape(NH, 128).T.copy()
    bcx_ = b_conv[0:DI].reshape(NH, 128).T.copy()
    bcbc_ = np.stack([b_conv[DI:DI + DS], b_conv[DI + DS:DI + 2 * DS]], axis=1)
    tpx_ = W_conv[0:DI, 0, :].reshape(NH, 128, 4).transpose(1, 0, 2).reshape(128, -1)
    tpbc_ = np.concatenate([W_conv[DI:DI + DS, 0, :],
                            W_conv[DI + DS:DI + 2 * DS, 0, :]], axis=1)

    import ml_dtypes
    bf = lambda x: np.ascontiguousarray(x).astype(ml_dtypes.bfloat16)

    shared = dict(
        w_z=np.ascontiguousarray(wz), w_x=np.ascontiguousarray(wxx),
        w_b=np.ascontiguousarray(wb), w_c=np.ascontiguousarray(wc),
        w_dt=np.ascontiguousarray(wdt), w_out=np.ascontiguousarray(wo),
        w_up=np.ascontiguousarray(wu), w_dn=np.ascontiguousarray(wd),
        w_ex=np.ascontiguousarray(we),
        c_identf=ident, c_identb=bf(ident), c_le=le, c_leb=bf(le),
        c_ones=np.ones((128, 128), np.float32),
        c_onesb=bf(np.ones((128, 128), np.float32)), c_sel=sel,
        c_small=small, c_drow=np.ascontiguousarray(D_sk[None, :].astype(np.float32)),
        c_fvals=np.tile(np.array([[EPS, 1.0, -1.0, 0.0, 1.0 / DI, 0.0, 0.0, 0.0]],
                        np.float32), (128, 1)),
        c_nw=np.ascontiguousarray(nwm),
        c_bcx=np.ascontiguousarray(bcx_), c_bcbc=np.ascontiguousarray(bcbc_),
        c_tpx=np.ascontiguousarray(tpx_), c_tpbc=np.ascontiguousarray(tpbc_),
    )
    in_maps = []
    for k in range(n_cores):
        s0 = k * T
        if k == 0:
            ctx = np.zeros((CTX, D), np.float32)
        else:
            ctx = u[s0 - CTX:s0]
        ue = np.ascontiguousarray(np.concatenate([ctx, u[s0:s0 + T]], axis=0))
        qm = np.zeros((n_cores, n_cores), np.float32)
        qm[:k, :] = 1.0
        cm = np.zeros((n_cores, 1), np.float32)
        cm[:k] = 1.0
        m = dict(shared)
        m.update(u_in=ue.astype(np.float32), c_qmask=qm, c_cmask=cm)
        in_maps.append(m)
    return in_maps


def kernel(**inputs):
    n_cores = 8
    L = np.asarray(inputs["L0_attn_u"]).shape[1]
    T = L // n_cores
    in_maps = _prep_inputs(inputs, T, n_cores)
    nc = build(T, n_cores)
    res = run_bass_kernel_spmd(nc, in_maps, core_ids=list(range(n_cores)))
    outs = [np.asarray(res.results[k]["out"], dtype=np.float32)
            for k in range(n_cores)]
    return np.concatenate(outs, axis=0)[None]


# version 02 (final)
# speedup vs baseline: 1.4220x; 1.4220x over previous
import numpy as np
from contextlib import ExitStack

import concourse.bass as bass
import concourse.mybir as mybir
from concourse.bass_utils import run_bass_kernel_spmd

F32 = mybir.dt.float32
BF16 = mybir.dt.bfloat16
AF = mybir.ActivationFunctionType
ALU = mybir.AluOpType

D = 2048
DI = 4096
NH = 32
HD = 128
DS = 64
CH = 128
EPS = 1e-5
CTX = 3
KD = D // 128
DEV_DUMPS = False


def ceil_div(a, b):
    return (a + b - 1) // b


class Ctr:
    def __init__(self, xs, nc, name, step=1):
        self.h = xs.enter_context(nc.semaphore(name))
        self.n = 0
        self.step = step

    def inc(self, instr, k=1):
        instr.then_inc(self.h, k * self.step)
        self.n += k * self.step
        return self.n


def build(T, n_cores):
    assert T % 512 == 0
    TE = T + CTX
    NCH = T // CH
    NT = ceil_div(TE, 128)
    NTR = T // 128
    n512 = T // 512
    NCb = n_cores

    nc = bass.Bass("TRN2", target_bir_lowering=False, debug=False,
                   detect_race_conditions=False)

    u_in = nc.dram_tensor("u_in", [TE, D], F32, kind="ExternalInput").ap()
    w_z = nc.dram_tensor("w_z", [NH, 128, KD * 128], F32, kind="ExternalInput").ap()
    w_x = nc.dram_tensor("w_x", [NH, 128, KD * 128], F32, kind="ExternalInput").ap()
    w_b = nc.dram_tensor("w_b", [128, KD * DS], F32, kind="ExternalInput").ap()
    w_c = nc.dram_tensor("w_c", [128, KD * DS], F32, kind="ExternalInput").ap()
    w_dt = nc.dram_tensor("w_dt", [128, KD * NH], F32, kind="ExternalInput").ap()
    w_out = nc.dram_tensor("w_out", [KD, 128, 32 * 128], F32, kind="ExternalInput").ap()
    w_up = nc.dram_tensor("w_up", [KD, 128, KD * 128], F32, kind="ExternalInput").ap()
    w_dn = nc.dram_tensor("w_dn", [KD, 128, KD * 128], F32, kind="ExternalInput").ap()
    w_ex = nc.dram_tensor("w_ex", [KD, 128, KD * 128], F32, kind="ExternalInput").ap()
    c_identf = nc.dram_tensor("c_identf", [128, 128], F32, kind="ExternalInput").ap()
    c_identb = nc.dram_tensor("c_identb", [128, 128], BF16, kind="ExternalInput").ap()
    c_le = nc.dram_tensor("c_le", [128, 128], F32, kind="ExternalInput").ap()
    c_leb = nc.dram_tensor("c_leb", [128, 128], BF16, kind="ExternalInput").ap()
    c_onesb = nc.dram_tensor("c_onesb", [128, 128], BF16, kind="ExternalInput").ap()
    c_ones = nc.dram_tensor("c_ones", [128, 128], F32, kind="ExternalInput").ap()
    c_sel = nc.dram_tensor("c_sel", [NCb, NCb * DS], F32, kind="ExternalInput").ap()
    c_qmask = nc.dram_tensor("c_qmask", [NCb, NCb], F32, kind="ExternalInput").ap()
    c_cmask = nc.dram_tensor("c_cmask", [NCb, 1], F32, kind="ExternalInput").ap()
    c_small = nc.dram_tensor("c_small", [NH, 4], F32, kind="ExternalInput").ap()
    c_drow = nc.dram_tensor("c_drow", [1, NH], F32, kind="ExternalInput").ap()
    c_fvals = nc.dram_tensor("c_fvals", [128, 8], F32, kind="ExternalInput").ap()
    c_nw = nc.dram_tensor("c_nw", [128, NH], F32, kind="ExternalInput").ap()
    c_bcx = nc.dram_tensor("c_bcx", [128, NH], F32, kind="ExternalInput").ap()
    c_bcbc = nc.dram_tensor("c_bcbc", [DS, 2], F32, kind="ExternalInput").ap()
    c_tpx = nc.dram_tensor("c_tpx", [128, NH * 4], F32, kind="ExternalInput").ap()
    c_tpbc = nc.dram_tensor("c_tpbc", [DS, 8], F32, kind="ExternalInput").ap()

    out = nc.dram_tensor("out", [T, D], F32, kind="ExternalOutput").ap()
    if DEV_DUMPS:
        dbg_lnu = nc.dram_tensor("dbg_lnu", [128, KD * TE], BF16, kind="ExternalOutput").ap()
        dbg_x0 = nc.dram_tensor("dbg_x0", [128, T], BF16, kind="ExternalOutput").ap()
        dbg_z0 = nc.dram_tensor("dbg_z0", [128, T], BF16, kind="ExternalOutput").ap()
        dbg_dt = nc.dram_tensor("dbg_dt", [NH, T], F32, kind="ExternalOutput").ap()
        dbg_gw0 = nc.dram_tensor("dbg_gw0", [128, T], BF16, kind="ExternalOutput").ap()
        dbg_sg2 = nc.dram_tensor("dbg_sg2", [1, T], F32, kind="ExternalOutput").ap()
        dbg_x1 = nc.dram_tensor("dbg_x1", [128, D], F32, kind="ExternalOutput").ap()
        dbg_x1n = nc.dram_tensor("dbg_x1n", [128, KD * T], BF16, kind="ExternalOutput").ap()
        dbg_h1 = nc.dram_tensor("dbg_h1", [128, KD * T], BF16, kind="ExternalOutput").ap()

    _sk = dict(kind="ExternalOutput") if DEV_DUMPS else {}
    zsil_d = nc.dram_tensor("zsil_d", [NH, 128, T], BF16, **_sk)
    xsil_d = nc.dram_tensor("xsil_d", [NH, 128, T], BF16, **_sk)
    gw_d = nc.dram_tensor("gw_d", [NH, 128, T], BF16, **_sk)
    x1_d = nc.dram_tensor("x1_d", [NTR, 128, D], F32, **_sk)
    loc_in = [nc.dram_tensor(f"loc_in{h}", [DS, 128], F32) for h in range(NH)]
    loc_out = [nc.dram_tensor(f"loc_out{h}", [NCb * DS, 128], F32) for h in range(NH)]
    lng_in = nc.dram_tensor("lng_in", [1, NH], F32)
    lng_out = nc.dram_tensor("lng_out", [NCb, NH], F32)

    xs = ExitStack()
    with xs:
        def sbuf(name, shape, dt):
            return xs.enter_context(nc.sbuf_tensor(name, shape, dt))

        identf = sbuf("identf", [128, 128], F32)
        identb = sbuf("identb", [128, 128], BF16)
        le = sbuf("le", [128, 128], F32)
        leb = sbuf("leb", [128, 128], BF16)
        onesb = sbuf("onesb", [128, 128], BF16)
        onesf = sbuf("onesf", [128, 128], F32)
        selt = sbuf("selt", [NCb, NCb * DS], F32)
        qmask = sbuf("qmask", [NCb, NCb], F32)
        cmask = sbuf("cmask", [NCb, 1], F32)
        small = sbuf("small", [NH, 4], F32)
        nwm = sbuf("nwm", [128, NH], F32)
        bcx = sbuf("bcx", [128, NH], F32)
        bcbc = sbuf("bcbc", [DS, 2], F32)
        tpx = sbuf("tpx", [128, NH * 4], F32)
        tpbc = sbuf("tpbc", [DS, 8], F32)
        aneg = sbuf("aneg", [NH, 1], F32)
        drep = sbuf("drep", [128, NH], F32)
        fvals = sbuf("fvals", [128, 8], F32)
        FV = [EPS, 1.0, -1.0, 0.0, 1.0 / DI]
        for fi, fv in enumerate(FV):
            nc.const_aps.register(F32, fv, fvals[:, fi:fi + 1])
        sg2 = sbuf("sg2", [1, T], F32)
        rr_row = sbuf("rr_row", [1, T], F32)
        lnw = sbuf("lnw", [128, 8], F32)
        lnmsq = sbuf("lnmsq", [128, 8], F32)

        sW = Ctr(xs, nc, "sW", 16)
        sIN = Ctr(xs, nc, "sIN", 16)
        sOUT = Ctr(xs, nc, "sOUT", 16)
        sT = Ctr(xs, nc, "sT")
        sV = Ctr(xs, nc, "sV")
        sS = Ctr(xs, nc, "sS")
        sG = Ctr(xs, nc, "sG", 16)
        sCC = Ctr(xs, nc, "sCC")

        block = xs.enter_context(nc.Block())
        SY, GP, TEN, VEC, SCA = block.sync, block.gpsimd, block.tensor, block.vector, block.scalar

        AX = mybir.AxisListType.X

        def emit_ln_aggr(bnb_t, bna_t, nr, wait_s=None):
            def _(e):
                if wait_s is not None:
                    e.wait_ge(sS.h, wait_s)
                means = bass.AP(bnb_t, 1, [[24, nr], [3, 8]])
                cvs = bass.AP(bnb_t, 2, [[24, nr], [3, 8]])
                e.reduce_sum(lnw[0:nr, 0:1], means, axis=AX)
                e.reduce_sum(lnw[0:nr, 1:2], cvs, axis=AX)
                sV.inc(e.tensor_mul(lnmsq[0:nr, 0:8], means, means))
                e.wait_ge(sV.h, sV.n)
                sV.inc(e.reduce_sum(lnw[0:nr, 2:3], lnmsq[0:nr, 0:8], axis=AX))
                e.wait_ge(sV.h, sV.n)
                e.tensor_scalar_mul(bna_t[0:nr, 0:1], lnw[0:nr, 0:1], 0.125)
                sV.inc(e.tensor_scalar_mul(lnw[0:nr, 3:4], lnw[0:nr, 1:2],
                                           1.0 / 2048.0))
                e.wait_ge(sV.h, sV.n)
                sV.inc(e.scalar_tensor_tensor(lnw[0:nr, 4:5], lnw[0:nr, 2:3],
                                              0.125, lnw[0:nr, 3:4],
                                              ALU.mult, ALU.add))
                e.wait_ge(sV.h, sV.n)
                sV.inc(e.tensor_mul(lnw[0:nr, 5:6], bna_t[0:nr, 0:1],
                                    bna_t[0:nr, 0:1]))
                e.wait_ge(sV.h, sV.n)
                sV.inc(e.tensor_sub(bna_t[0:nr, 1:2], lnw[0:nr, 4:5],
                                    lnw[0:nr, 5:6]))
            VEC(_)

        def _(e):
            for dst, src in [(identf, c_identf), (identb, c_identb), (le, c_le),
                             (leb, c_leb), (onesb, c_onesb), (onesf, c_ones), (selt, c_sel),
                             (qmask, c_qmask), (cmask, c_cmask), (small, c_small),
                             (nwm, c_nw), (bcx, c_bcx), (bcbc, c_bcbc),
                             (tpx, c_tpx), (tpbc, c_tpbc), (fvals, c_fvals)]:
                sIN.inc(e.dma_start(out=dst[:, :], in_=src))
        SY(_)
        n_const = sIN.n

        drow = sbuf("drow", [1, NH], F32)

        def _(e):
            sIN.inc(e.dma_start(out=drow[:, :], in_=c_drow))
        SY(_)
        n_const = sIN.n

        with nc.psum_tensor("p_p0", [128, 512], F32) as p_p0:
            def _(e):
                e.wait_ge(sIN.h, n_const)
                sS.inc(e.activation(aneg[:, 0:1], small[:, 1:2], AF.Exp))
            SCA(_)
            w1 = sS.n

            def _(e):
                e.wait_ge(sS.h, w1)
                sV.inc(e.tensor_scalar_mul(aneg[:, 0:1], aneg[:, 0:1], -1.0))
            VEC(_)

            def _(e):
                e.wait_ge(sIN.h, n_const)
                sT.inc(e.matmul(p_p0[:, 0:NH], onesf[0:1, :], drow[0:1, :],
                                start=True, stop=True))
            TEN(_)
            w4 = sT.n

            def _(e):
                e.wait_ge(sT.h, w4)
                sV.inc(e.tensor_copy(drep[:, :], p_p0[:, 0:NH]))
            VEC(_)

        with ExitStack() as ms:
            def msb(name, shape, dt):
                return ms.enter_context(nc.sbuf_tensor(name, shape, dt))

            c_fm = msb("c_fm", [DS, T], BF16)
            ac_tm = msb("ac_tm", [128, NCH * NH], F32)
            e_tm = msb("e_tm", [128, NCH * NH], F32)
            dtsp_tm = msb("dtsp_tm", [128, NCH * NH], F32)
            rcdt_tm = msb("rcdt_tm", [128, NCH * NH], F32)
            b_tm = msb("b_tm", [128, NCH * DS], BF16)
            gmt = msb("gmt", [128, NCH * 128], BF16)
            grep = msb("grep", [DS, NCH * NH], F32)
            lngr = msb("lngr", [1, NH], F32)
            coefrep = msb("coefrep", [DS, NCb * NH], F32)
            b_fm = msb("b_fm", [DS, T], BF16)
            dtsp = msb("dtsp", [NH, T], F32)
            adt = msb("adt", [NH, T], F32)

            with ExitStack() as a0:
                def a0b_(name, shape, dt):
                    return a0.enter_context(nc.sbuf_tensor(name, shape, dt))

                lnu = a0b_("lnu", [128, KD * TE], BF16)
                u_t = [a0b_("u_t0", [128, D], F32), a0b_("u_t1", [128, D], F32)]
                lnt = [a0b_("lnt0", [128, D], BF16), a0b_("lnt1", [128, D], BF16)]
                bnb = a0b_("bnb", [128, 24], F32)
                bna = a0b_("bna", [128, 2], F32)
                rstd = a0b_("rstd", [128, 1], F32)
                nmr = a0b_("nmr", [128, 1], F32)

                with nc.psum_tensor("p_tr", [128, 512], BF16) as p_tr:
                    ev_done = sV.n
                    ut_free = [[sV.n, sS.n], [sV.n, sS.n]]
                    lnt_free = [sT.n, sT.n]
                    prev_norm_s = sS.n
                    for it in range(NT):
                        r0 = it * 128
                        nr = min(128, TE - r0)
                        pb = it % 2
                        ut = u_t[pb]
                        lt = lnt[pb]

                        def _(e, r0=r0, nr=nr, ut=ut, uf=tuple(ut_free[pb])):
                            e.wait_ge(sV.h, uf[0])
                            e.wait_ge(sS.h, uf[1])
                            sIN.inc(e.dma_start(out=ut[0:nr, :], in_=u_in[r0:r0 + nr, :]))
                        SY(_)
                        w_ld = sIN.n

                        def _(e, nr=nr, w_ld=w_ld, ut=ut):
                            e.wait_ge(sIN.h, w_ld)
                            for q in range(4):
                                ins = e.bn_stats(bnb[0:nr, 6 * q:6 * q + 6],
                                                 ut[0:nr, 512 * q:512 * (q + 1)])
                            sV.inc(ins)
                            e.wait_ge(sV.h, sV.n)
                        VEC(_)
                        emit_ln_aggr(bnb, bna, nr, wait_s=prev_norm_s)
                        w_st = sV.n

                        def _(e, nr=nr, w_st=w_st):
                            e.wait_ge(sV.h, w_st)
                            sS.inc(e.activation(rstd[0:nr, :], bna[0:nr, 1:2], AF.Sqrt,
                                                bias=EPS))
                        SCA(_)
                        w_sq = sS.n

                        def _(e, nr=nr, w_sq=w_sq):
                            e.wait_ge(sS.h, w_sq)
                            sV.inc(e.reciprocal(rstd[0:nr, :], rstd[0:nr, :]))
                            e.wait_ge(sV.h, sV.n)
                            sV.inc(e.tensor_scalar(nmr[0:nr, :], bna[0:nr, 0:1],
                                                   rstd[0:nr, 0:1], -1.0,
                                                   ALU.mult, ALU.mult))
                        VEC(_)
                        w_rs = sV.n

                        def _(e, nr=nr, w_rs=w_rs, ut=ut, lt=lt, lf=lnt_free[pb]):
                            e.wait_ge(sV.h, w_rs)
                            e.wait_ge(sT.h, lf)
                            sS.inc(e.activation(lt[0:nr, :], ut[0:nr, :], AF.Identity,
                                                bias=nmr[0:nr, 0:1],
                                                scale=rstd[0:nr, 0:1]))
                        SCA(_)
                        w_no = sS.n
                        prev_norm_s = w_no
                        ut_free[pb] = [w_st, w_no]

                        for k4 in range(0, KD, 4):
                            def _(e, nr=nr, k4=k4, w_no=w_no, ev=ev_done, lt=lt):
                                e.wait_ge(sS.h, w_no)
                                e.wait_ge(sV.h, ev)
                                for kk in range(k4, k4 + 4):
                                    ins = e.transpose(
                                        p_tr[:, (kk - k4) * 128:(kk - k4) * 128 + nr],
                                        lt[0:nr, kk * 128:(kk + 1) * 128],
                                        identb[0:nr, 0:nr])
                                sT.inc(ins)
                            TEN(_)
                            w_tt = sT.n

                            def _(e, nr=nr, k4=k4, r0=r0, w_tt=w_tt):
                                e.wait_ge(sT.h, w_tt)
                                for kk in range(k4, k4 + 4):
                                    ins = e.tensor_copy(
                                        lnu[:, kk * TE + r0:kk * TE + r0 + nr],
                                        p_tr[:, (kk - k4) * 128:(kk - k4) * 128 + nr])
                                sV.inc(ins)
                            VEC(_)
                            ev_done = sV.n
                        lnt_free[pb] = sT.n

                wt = [a0b_("wt0", [128, KD * 128], BF16),
                      a0b_("wt1", [128, KD * 128], BF16)]
                xw = a0b_("xw", [128, TE], F32)
                cva = a0b_("cva", [128, T], F32)
                sil = [a0b_("sil0", [128, T], BF16), a0b_("sil1", [128, T], BF16)]

                lnu_ready = sV.n

                sched = [("dt", w_dt, NH), ("B", w_b, DS), ("C", w_c, DS)]
                sched += [(f"z{h}", w_z[h], 128) for h in range(NH)]
                sched += [(f"x{h}", w_x[h], 128) for h in range(NH)]

                def _(e):
                    sW.inc(e.dma_start(out=wt[0][:, 0:KD * NH], in_=w_dt))
                GP(_)

                with nc.psum_tensor("p_m0", [128, 512], F32) as p_m0, \
                     nc.psum_tensor("p_m1", [128, 512], F32) as p_m1:
                    pfree = [sV.n, sV.n]
                    mm_end = {}
                    sil_free = [sOUT.n, sOUT.n]
                    cva_free_s = sS.n
                    for bi, (nm, wap, rows) in enumerate(sched):
                        kind = nm[0] if nm[0] in "zx" else nm
                        h = int(nm[1:]) if kind in "zx" else 0
                        buf = wt[bi % 2]
                        w_have = sW.n
                        if bi + 1 < len(sched):
                            nm2, wap2, rows2 = sched[bi + 1]
                            nbuf = wt[(bi + 1) % 2]
                            wprev = mm_end.get(bi - 1, 0)

                            def _(e, wap2=wap2, rows2=rows2, nbuf=nbuf, wprev=wprev):
                                e.wait_ge(sT.h, wprev)
                                sW.inc(e.dma_start(out=nbuf[:, 0:KD * rows2], in_=wap2))
                            GP(_)

                        ctx_cols = kind in ("B", "C", "x")
                        cols = TE if ctx_cols else T
                        c0 = 0 if ctx_cols else CTX
                        groups = []
                        g0 = 0
                        while g0 < cols:
                            groups.append((g0, min(512, cols - g0)))
                            g0 += 512
                        ev_marks = []
                        for gi, (g0, gn) in enumerate(groups):
                            pm = [p_m0, p_m1][gi % 2]

                            def _(e, rows=rows, g0=g0, gn=gn, c0=c0, buf=buf, pm=pm,
                                  w_have=w_have, pf=pfree[gi % 2], lr=lnu_ready):
                                e.wait_ge(sW.h, w_have)
                                e.wait_ge(sV.h, max(pf, lr))
                                for kk in range(KD):
                                    ins = e.matmul(
                                        pm[0:rows, 0:gn],
                                        buf[:, kk * rows:(kk + 1) * rows],
                                        lnu[:, kk * TE + c0 + g0:kk * TE + c0 + g0 + gn],
                                        start=(kk == 0), stop=(kk == KD - 1))
                                sT.inc(ins)
                            TEN(_)
                            mm_end[bi] = sT.n
                            w_mm = sT.n

                            if kind == "dt":
                                def _(e, g0=g0, gn=gn, pm=pm, w_mm=w_mm):
                                    e.wait_ge(sT.h, w_mm)
                                    sV.inc(e.tensor_copy(dtsp[:, g0:g0 + gn],
                                                         pm[0:NH, 0:gn]))
                                VEC(_)
                            elif kind in ("B", "C"):
                                dst = xw

                                def _(e, g0=g0, gn=gn, pm=pm, w_mm=w_mm):
                                    e.wait_ge(sT.h, w_mm)
                                    sV.inc(e.tensor_copy(xw[0:DS, g0:g0 + gn],
                                                         pm[0:DS, 0:gn]))
                                VEC(_)
                            elif kind == "z":
                                def _(e, g0=g0, gn=gn, pm=pm, w_mm=w_mm, h=h,
                                      sf=sil_free[h % 2]):
                                    e.wait_ge(sT.h, w_mm)
                                    e.wait_ge(sOUT.h, sf)
                                    sS.inc(e.activation(sil[h % 2][:, g0:g0 + gn],
                                                        pm[:, 0:gn], AF.Silu))
                                SCA(_)
                            else:
                                def _(e, g0=g0, gn=gn, pm=pm, w_mm=w_mm):
                                    e.wait_ge(sT.h, w_mm)
                                    sV.inc(e.tensor_copy(xw[:, g0:g0 + gn],
                                                         pm[:, 0:gn]))
                                VEC(_)
                            if kind in ("dt", "B", "C", "x"):
                                pfree[gi % 2] = sV.n
                            else:
                                def _(e, w_s=sS.n):
                                    e.wait_ge(sS.h, w_s)
                                    sV.inc(e.engine_nop())
                                VEC(_)
                                pfree[gi % 2] = sV.n

                        ev_all = max(pfree)
                        if kind == "dt":
                            def _(e, ev=pfree[:]):
                                e.wait_ge(sV.h, max(ev))
                                sV.inc(e.tensor_scalar_add(dtsp[:, :], dtsp[:, :],
                                                           small[:, 0:1]))
                                e.wait_ge(sV.h, sV.n)
                                sV.inc(e.tensor_relu(cva[0:NH, 0:T], dtsp[:, :]))
                            VEC(_)
                            w_v = sV.n

                            def _(e, w_v=w_v):
                                e.wait_ge(sV.h, w_v)
                                e.activation(xw[0:NH, 0:T], dtsp[:, :], AF.Abs)
                                e.activation(xw[0:NH, 0:T], xw[0:NH, 0:T],
                                             AF.Exp, scale=-1.0)
                                sS.inc(e.activation(xw[0:NH, 0:T], xw[0:NH, 0:T],
                                                    AF.Ln, bias=1.0))
                            SCA(_)
                            w_s = sS.n

                            def _(e, w_s=w_s):
                                e.wait_ge(sS.h, w_s)
                                sV.inc(e.tensor_add(dtsp[:, :], xw[0:NH, 0:T],
                                                    cva[0:NH, 0:T]))
                                e.wait_ge(sV.h, sV.n)
                                sV.inc(e.tensor_scalar_mul(adt[:, :], dtsp[:, :],
                                                           aneg[:, 0:1]))
                            VEC(_)
                        elif kind in ("B", "C"):
                            ti = 0 if kind == "B" else 1

                            def _(e, ev=ev_all, ti=ti, cf=cva_free_s):
                                e.wait_ge(sV.h, ev)
                                e.wait_ge(sS.h, cf)
                                e.tensor_scalar_mul(cva[0:DS, 0:T], xw[0:DS, 0:T],
                                                    tpbc[:, 4 * ti:4 * ti + 1])
                                for j in range(1, 4):
                                    ins = e.scalar_tensor_tensor(
                                        cva[0:DS, 0:T], xw[0:DS, j:T + j],
                                        tpbc[:, 4 * ti + j:4 * ti + j + 1],
                                        cva[0:DS, 0:T], ALU.mult, ALU.add)
                                sV.inc(ins)
                            VEC(_)
                            w_cv = sV.n
                            dstt = b_fm if kind == "B" else c_fm

                            def _(e, w_cv=w_cv, dstt=dstt, ti=ti):
                                e.wait_ge(sV.h, w_cv)
                                sS.inc(e.activation(dstt[:, :], cva[0:DS, 0:T],
                                                    AF.Silu, bias=bcbc[:, ti:ti + 1]))
                            SCA(_)
                            cva_free_s = sS.n
                        elif kind == "z":
                            def _(e, h=h, w_s=sS.n):
                                e.wait_ge(sS.h, w_s)
                                sOUT.inc(e.dma_start(out=zsil_d[h], in_=sil[h % 2][:, :]))
                                if DEV_DUMPS and h == 0:
                                    sOUT.inc(e.dma_start(out=dbg_z0, in_=sil[0][:, :]))
                                    sOUT.inc(e.dma_start(out=dbg_lnu, in_=lnu[:, :]))
                                    sOUT.inc(e.dma_start(out=dbg_dt, in_=dtsp[:, :]))
                            SCA(_)
                            sil_free[h % 2] = sOUT.n
                        else:
                            def _(e, ev=ev_all, h=h, cf=cva_free_s):
                                e.wait_ge(sV.h, ev)
                                e.wait_ge(sS.h, cf)
                                e.tensor_scalar_mul(cva[:, :], xw[:, 0:T],
                                                    tpx[:, 4 * h:4 * h + 1])
                                for j in range(1, 4):
                                    ins = e.scalar_tensor_tensor(
                                        cva[:, :], xw[:, j:T + j],
                                        tpx[:, 4 * h + j:4 * h + j + 1],
                                        cva[:, :], ALU.mult, ALU.add)
                                sV.inc(ins)
                            VEC(_)
                            w_cv = sV.n

                            def _(e, w_cv=w_cv, h=h, sf=sil_free[h % 2]):
                                e.wait_ge(sV.h, w_cv)
                                e.wait_ge(sOUT.h, sf)
                                sS.inc(e.activation(sil[h % 2][:, :], cva[:, :],
                                                    AF.Silu, bias=bcx[:, h:h + 1]))
                            SCA(_)
                            cva_free_s = sS.n
                            w_si = sS.n

                            def _(e, w_si=w_si, h=h):
                                e.wait_ge(sS.h, w_si)
                                sOUT.inc(e.dma_start(out=xsil_d[h], in_=sil[h % 2][:, :]))
                                if DEV_DUMPS and h == 0:
                                    sOUT.inc(e.dma_start(out=dbg_x0, in_=sil[0][:, :]))
                            SCA(_)
                            sil_free[h % 2] = sOUT.n

            with nc.psum_tensor("p_a", [128, 512], F32) as p_a, \
                 nc.psum_tensor("p_b", [128, 512], F32) as p_b, \
                 nc.psum_tensor("p_ab", [128, 512], BF16) as p_ab, \
                 nc.sbuf_tensor("adtw", [128, NH], F32) as adtw, \
                 nc.sbuf_tensor("aclr", [NH, DS], F32) as aclr, \
                 nc.sbuf_tensor("acw", [NH, CH], F32) as acw, \
                 nc.sbuf_tensor("rcw", [NH, CH], F32) as rcw, \
                 nc.sbuf_tensor("grow", [1, NH], F32) as grow:
                pa_v = sV.n
                pa_s = sS.n
                pb_s = sS.n
                a0_spills = sOUT.n
                for c in range(NCH):
                    cs = slice(c * CH, (c + 1) * CH)
                    ch_ = slice(c * NH, (c + 1) * NH)
                    cd = slice(c * DS, (c + 1) * DS)

                    def _(e, c=c, pa=pa_v, pas=pa_s, sp=a0_spills):
                        e.wait_ge(sV.h, pa)
                        e.wait_ge(sS.h, pas)
                        e.wait_ge(sOUT.h, sp)
                        e.transpose(p_a[:, 0:NH], adt[:, c * CH:(c + 1) * CH],
                                    identf[0:NH, 0:NH])
                        sT.inc(e.transpose(p_a[:, NH:2 * NH],
                                           dtsp[:, c * CH:(c + 1) * CH],
                                           identf[0:NH, 0:NH]))
                    TEN(_)
                    w_t1 = sT.n

                    def _(e, ch_=ch_, w_t1=w_t1):
                        e.wait_ge(sT.h, w_t1)
                        e.tensor_copy(adtw[:, :], p_a[:, 0:NH])
                        sV.inc(e.tensor_copy(dtsp_tm[:, ch_], p_a[:, NH:2 * NH]))
                    VEC(_)
                    w_v1 = sV.n

                    def _(e, c=c, w_v1=w_v1, pbs=pb_s):
                        e.wait_ge(sV.h, w_v1)
                        e.wait_ge(sS.h, pbs)
                        e.matmul(p_a[0:NH, 0:CH], adtw[:, :], le[:, :],
                                 start=True, stop=True)
                        sT.inc(e.matmul(p_b[:, 0:NH], le[:, :], adtw[:, :],
                                        start=True, stop=True))
                    TEN(_)
                    w_t2 = sT.n

                    def _(e, c=c, ch_=ch_, w_t2=w_t2):
                        e.wait_ge(sT.h, w_t2)
                        sV.inc(e.tensor_copy(acw[:, :], p_a[0:NH, 0:CH]))
                        sV.inc(e.tensor_copy(ac_tm[:, ch_], p_b[:, 0:NH]))
                        e.wait_ge(sV.h, sV.n)
                    VEC(_)
                    w_v2 = sV.n

                    def _(e, ch_=ch_, w_v2=w_v2):
                        e.wait_ge(sV.h, w_v2)
                        sS.inc(e.activation(e_tm[:, ch_], p_b[:, 0:NH], AF.Exp))
                    SCA(_)
                    pb_free0 = sS.n
                    pb_s = sS.n

                    def _(e, c=c, w_v2=w_v2):
                        e.wait_ge(sV.h, w_v2)
                        sV.inc(e.tensor_scalar_mul(
                            aclr[:, :], onesf[0:NH, 0:DS],
                            acw[:, CH - 1:CH]))
                    VEC(_)
                    w_vg = sV.n

                    def _(e, w_vg=w_vg, pas=pa_s):
                        e.wait_ge(sV.h, w_vg)
                        e.wait_ge(sS.h, pas)
                        sT.inc(e.transpose(p_a[0:DS, 256:256 + NH], aclr[:, :],
                                           identf[0:NH, 0:NH]))
                    TEN(_)
                    w_t4b = sT.n

                    def _(e, c=c, w_t4b=w_t4b):
                        e.wait_ge(sT.h, w_t4b)
                        sV.inc(e.tensor_copy(grow[:, :], p_a[0:1, 256:256 + NH]))
                        e.wait_ge(sV.h, sV.n)
                        if c == 0:
                            ins = e.tensor_copy(lngr[:, :], grow[:, :])
                        else:
                            ins = e.tensor_add(lngr[:, :], lngr[:, :], grow[:, :])
                        sV.inc(ins)
                    VEC(_)

                    def _(e, c=c, w_v2=w_v2):
                        e.wait_ge(sV.h, w_v2)
                        sV.inc(e.tensor_scalar(
                            rcw[:, :], acw[:, :],
                            acw[:, CH - 1:CH], None,
                            ALU.subtract, ALU.bypass))
                    VEC(_)
                    w_v3 = sV.n

                    def _(e, w_v3=w_v3):
                        e.wait_ge(sV.h, w_v3)
                        sS.inc(e.activation(rcw[:, :], rcw[:, :], AF.Exp, scale=-1.0))
                    SCA(_)
                    w_s1 = sS.n

                    def _(e, w_s1=w_s1, pbf=pb_free0):
                        e.wait_ge(sS.h, max(w_s1, pbf))
                        sT.inc(e.transpose(p_b[:, 0:NH], rcw[:, :],
                                           identf[0:NH, 0:NH]))
                    TEN(_)
                    w_t3 = sT.n

                    def _(e, ch_=ch_, w_t3=w_t3):
                        e.wait_ge(sT.h, w_t3)
                        sV.inc(e.tensor_mul(rcdt_tm[:, ch_], p_b[:, 0:NH],
                                            dtsp_tm[:, ch_]))
                    VEC(_)

                    def _(e, ch_=ch_, w_t4b=w_t4b):
                        e.wait_ge(sT.h, w_t4b)
                        sS.inc(e.activation(grep[:, ch_], p_a[0:DS, 256:256 + NH],
                                            AF.Exp))
                    SCA(_)
                    pa_s = sS.n

                    def _(e, c=c, cs=cs, w_prev=sV.n):
                        e.wait_ge(sV.h, w_prev)
                        e.transpose(p_ab[:, 0:DS], b_fm[:, cs],
                                    identb[0:DS, 0:DS])
                        sT.inc(e.matmul(p_b[:, 128:256], b_fm[:, cs], c_fm[:, cs],
                                        start=True, stop=True))
                    TEN(_)
                    w_t5 = sT.n

                    def _(e, c=c, cs=cs, cd=cd, w_t5=w_t5):
                        e.wait_ge(sT.h, w_t5)
                        e.tensor_copy(b_tm[:, cd], p_ab[:, 0:DS])
                        sV.inc(e.tensor_mul(gmt[:, cs], p_b[:, 128:256], leb[:, :]))
                    VEC(_)
                    pa_v = sV.n

            with nc.psum_tensor("p_q", [128, 512], F32) as p_q, \
                 nc.sbuf_tensor("lgs", [NCb, NH], F32) as lgs, \
                 nc.sbuf_tensor("q_sb", [NCb, NH], F32) as q_sb, \
                 nc.sbuf_tensor("qk1", [1, NH], F32) as qk1, \
                 nc.sbuf_tensor("coef", [NCb, NH], F32) as coef:
                w_lng = sV.n

                def _(e, w_lng=w_lng):
                    e.wait_ge(sV.h, w_lng)
                    sG.inc(e.dma_start(out=lng_in.ap(), in_=lngr[:, :]))
                GP(_)
                w_g = sG.n

                def _(e, w_g=w_g):
                    e.wait_ge(sG.h, w_g)
                    sCC.inc(e.collective_compute(
                        "AllGather", ALU.bypass,
                        replica_groups=[list(range(NCb))],
                        ins=[lng_in.ap().opt()], outs=[lng_out.ap().opt()]))
                GP(_)
                w_cc = sCC.n

                def _(e, w_cc=w_cc):
                    e.wait_ge(sCC.h, w_cc)
                    sIN.inc(e.dma_start(out=lgs[:, :], in_=lng_out.ap()))
                SY(_)
                w_lgs = sIN.n

                def _(e, w_lgs=w_lgs):
                    e.wait_ge(sIN.h, w_lgs)
                    e.matmul(p_q[0:NCb, 0:NH], le[0:NCb, 0:NCb], lgs[:, :],
                             start=True, stop=True)
                    sT.inc(e.matmul(p_q[0:NCb, 2 * NH:3 * NH], qmask[:, :],
                                    lgs[:, :], start=True, stop=True))
                TEN(_)
                w_tq = sT.n

                def _(e, w_tq=w_tq):
                    e.wait_ge(sT.h, w_tq)
                    sV.inc(e.tensor_copy(q_sb[:, :], p_q[0:NCb, 0:NH]))
                    e.wait_ge(sV.h, sV.n)
                VEC(_)
                w_vq = sV.n

                def _(e, w_vq=w_vq):
                    e.wait_ge(sV.h, w_vq)
                    sV.inc(e.tensor_sub(coef[:, :], p_q[0:NCb, 2 * NH:3 * NH],
                                        q_sb[:, :]))
                    e.wait_ge(sV.h, sV.n)
                    sV.inc(e.tensor_scalar_min(coef[:, :], coef[:, :], 0.0))
                VEC(_)
                w_vc = sV.n

                def _(e, w_vc=w_vc):
                    e.wait_ge(sV.h, w_vc)
                    sS.inc(e.activation(coef[:, :], coef[:, :], AF.Exp))
                SCA(_)
                w_sc = sS.n

                def _(e, w_sc=w_sc):
                    e.wait_ge(sS.h, w_sc)
                    sV.inc(e.tensor_scalar_mul(coef[:, :], coef[:, :], cmask[:, 0:1]))
                VEC(_)
                w_vm = sV.n

                for kp in range(NCb):
                    def _(e, kp=kp, w_vm=w_vm, pvf=sV.n):
                        e.wait_ge(sV.h, max(w_vm, pvf))
                        sT.inc(e.matmul(p_q[0:DS, 3 * NH + 0:3 * NH + NH],
                                        selt[:, kp * DS:(kp + 1) * DS], coef[:, :],
                                        start=True, stop=True))
                    TEN(_)
                    w_tk = sT.n

                    def _(e, kp=kp, w_tk=w_tk):
                        e.wait_ge(sT.h, w_tk)
                        sV.inc(e.tensor_copy(coefrep[:, kp * NH:(kp + 1) * NH],
                                             p_q[0:DS, 3 * NH:4 * NH]))
                    VEC(_)

            with ExitStack() as hl:
                def hsb(name, shape, dt):
                    return hl.enter_context(nc.sbuf_tensor(name, shape, dt))

                xsb2 = [hsb("xsb0", [128, T], BF16), hsb("xsb1", [128, T], BF16)]
                zsb2 = [hsb("zsb0", [128, T], BF16), hsb("zsb1", [128, T], BF16)]
                x_tm2 = [hsb("x_tm0", [128, NCH * 128], BF16),
                         hsb("x_tm1", [128, NCH * 128], BF16)]
                sraw2 = [hsb("sraw0", [DS, NCH * 128], F32),
                         hsb("sraw1", [DS, NCH * 128], F32)]
                bw_all = hsb("bw_all", [128, NCH * DS], BF16)
                loc_t = hsb("loc_t", [DS, 128], F32)
                bk = [hsb("bk0", [DS, 128], F32), hsb("bk1", [DS, 128], F32)]
                sinb_all = hsb("sinb_all", [DS, NCH * 128], BF16)
                gath = hsb("gath", [DS, NCb * 128], F32)
                crep4 = hsb("crep4", [128, 512], F32)
                rr4 = hsb("rr4", [128, 512], F32)
                texp4 = hsb("texp4", [128, 512], BF16)
                mt4 = hsb("mt4", [128, 512], BF16)
                yt4 = hsb("yt4", [128, 512], F32)
                ym4 = hsb("ym4", [128, 512], F32)
                tf4 = hsb("tf4", [128, 512], F32)
                g_all = hsb("g_all", [128, T], BF16)
                g2b = hsb("g2b", [128, T], BF16)
                gwb = hsb("gwb", [128, T], BF16)

                NB = NCH // 4

                with nc.psum_tensor("p_x", [128, 1024], BF16) as p_x, \
                     nc.psum_tensor("p_s0", [128, 512], F32) as p_s0, \
                     nc.psum_tensor("p_s1", [128, 512], F32) as p_s1, \
                     nc.psum_tensor("p_rt", [128, 512], F32) as p_rt, \
                     nc.psum_tensor("p_y", [128, 512], F32) as p_y, \
                     nc.psum_tensor("p_yo", [128, 512], F32) as p_yo, \
                     nc.psum_tensor("p_yf", [128, 512], F32) as p_yf, \
                     nc.psum_tensor("p_sg", [128, 512], F32) as p_sg:

                    px_free = sV.n
                    ps_free = [sV.n, sV.n]
                    crep_free = sT.n
                    prt_free = sS.n
                    texp_free = sV.n
                    mt_free = sT.n
                    pyy_free = sV.n
                    pyf_free = sV.n
                    ym_free = sT.n
                    tf_free = sV.n
                    psg_free = sV.n
                    sinb_free = sT.n
                    loc_free = sG.n
                    gath_free = sV.n
                    xz_free = [[sV.n, sT.n], [sV.n, sT.n]]
                    spill_prev = sOUT.n
                    last_mm_mark = sT.n
                    ag_mark = {}
                    st1_v = {}

                    def _(e, sp=spill_prev, bv=sV.n, bt=sT.n, bs=sS.n):
                        e.wait_ge(sOUT.h, sp)
                        e.wait_ge(sV.h, bv)
                        e.wait_ge(sT.h, bt)
                        e.wait_ge(sS.h, bs)
                        sIN.inc(e.dma_start(out=xsb2[0][:, :], in_=xsil_d[0]))
                        sIN.inc(e.dma_start(out=zsb2[0][:, :], in_=zsil_d[0]))
                    SY(_)
                    xz_ld = {0: sIN.n}

                    for hh in range(NH + 1):
                        if hh < NH:
                            h = hh
                            pb = h % 2
                            xsb = xsb2[pb]
                            x_tm = x_tm2[pb]
                            sraw = sraw2[pb]

                            H1 = min(8, NCH)

                            def _(e, h=h, xsb=xsb, w_xz=xz_ld[h], pxf=px_free,
                                  H1=H1):
                                e.wait_ge(sIN.h, w_xz)
                                e.wait_ge(sV.h, pxf)
                                for c in range(H1):
                                    ins = e.transpose(
                                        p_x[:, c * 128:(c + 1) * 128],
                                        xsb[:, c * CH:(c + 1) * CH], identb[:, :])
                                sT.inc(ins)
                            TEN(_)
                            w_tx0 = sT.n

                            def _(e, h=h, x_tm=x_tm, w_tx0=w_tx0, H1=H1):
                                e.wait_ge(sT.h, w_tx0)
                                for c in range(H1):
                                    ins = e.tensor_copy(
                                        x_tm[:, c * 128:(c + 1) * 128],
                                        p_x[:, c * 128:(c + 1) * 128])
                                sV.inc(ins)
                            VEC(_)
                            w_vx0 = sV.n

                            if NCH > H1:
                                def _(e, h=h, xsb=xsb, w_vx0=w_vx0, H1=H1):
                                    e.wait_ge(sV.h, w_vx0)
                                    for c in range(H1, NCH):
                                        ins = e.transpose(
                                            p_x[:, (c - H1) * 128:(c - H1 + 1) * 128],
                                            xsb[:, c * CH:(c + 1) * CH],
                                            identb[:, :])
                                    sT.inc(ins)
                                TEN(_)
                            w_tx1 = sT.n

                            def _(e, h=h, x_tm=x_tm, w_tx1=w_tx1, H1=H1):
                                e.wait_ge(sT.h, w_tx1)
                                for c in range(H1, NCH):
                                    e.tensor_copy(
                                        x_tm[:, c * 128:(c + 1) * 128],
                                        p_x[:, (c - H1) * 128:(c - H1 + 1) * 128])
                                for c in range(NCH):
                                    col = c * NH + h
                                    ins = e.tensor_scalar_mul(
                                        bw_all[:, c * DS:(c + 1) * DS],
                                        b_tm[:, c * DS:(c + 1) * DS],
                                        rcdt_tm[:, col:col + 1])
                                sV.inc(ins)
                            VEC(_)
                            px_free = sV.n
                            w_bw = sV.n

                            ev_marks = []
                            for g in range(NCH // 4):
                                ps = [p_s0, p_s1][g % 2]

                                def _(e, g=g, ps=ps, x_tm=x_tm, w_bw=w_bw,
                                      psf=ps_free[g % 2]):
                                    e.wait_ge(sV.h, max(w_bw, psf))
                                    for j in range(4):
                                        c = g * 4 + j
                                        ins = e.matmul(
                                            ps[0:DS, j * 128:(j + 1) * 128],
                                            bw_all[:, c * DS:(c + 1) * DS],
                                            x_tm[:, c * 128:(c + 1) * 128],
                                            start=True, stop=True)
                                    sT.inc(ins)
                                TEN(_)
                                w_ts = sT.n

                                def _(e, g=g, ps=ps, sraw=sraw, w_ts=w_ts):
                                    e.wait_ge(sT.h, w_ts)
                                    ins = e.tensor_copy(
                                        sraw[:, g * 512:(g + 1) * 512],
                                        ps[0:DS, 0:512])
                                    sV.inc(ins)
                                VEC(_)
                                ps_free[g % 2] = sV.n
                            w_sraw = sV.n

                            def _(e, h=h, sraw=sraw, w_sraw=w_sraw, lf=loc_free):
                                e.wait_ge(sV.h, w_sraw)
                                e.wait_ge(sG.h, lf)
                                ins = e.tensor_copy(loc_t[:, :], sraw[:, 0:128])
                                for c in range(1, NCH):
                                    col = c * NH + h
                                    ins = e.scalar_tensor_tensor(
                                        loc_t[:, :], loc_t[:, :],
                                        grep[:, col:col + 1],
                                        sraw[:, c * 128:(c + 1) * 128],
                                        ALU.mult, ALU.add)
                                sV.inc(ins)
                            VEC(_)
                            loc_mark = sV.n
                            st1_v[h] = sV.n

                            def _(e, h=h, lm=loc_mark):
                                e.wait_ge(sV.h, lm)
                                sG.inc(e.dma_start(out=loc_in[h].ap(),
                                                   in_=loc_t[:, :]))
                            GP(_)
                            loc_free = sG.n
                            w_gl = sG.n

                            def _(e, h=h, w_gl=w_gl):
                                e.wait_ge(sG.h, w_gl)
                                sCC.inc(e.collective_compute(
                                    "AllGather", ALU.bypass,
                                    replica_groups=[list(range(NCb))],
                                    ins=[loc_in[h].ap().opt()],
                                    outs=[loc_out[h].ap().opt()]))
                            GP(_)
                            ag_mark[h] = sCC.n

                        if hh >= 1:
                            h = hh - 1
                            pb = h % 2
                            xsb = xsb2[pb]
                            zsb = zsb2[pb]
                            x_tm = x_tm2[pb]
                            sraw = sraw2[pb]

                            def _(e, h=h, w_cch=ag_mark[h], gf=gath_free):
                                e.wait_ge(sCC.h, w_cch)
                                e.wait_ge(sV.h, gf)
                                for kp in range(NCb):
                                    sIN.inc(e.dma_start(
                                        out=gath[:, kp * 128:(kp + 1) * 128],
                                        in_=loc_out[h].ap()[kp * DS:(kp + 1) * DS, :]))
                            SY(_)
                            w_ga = sIN.n

                            def _(e, h=h, w_ga=w_ga, sbf=sinb_free):
                                e.wait_ge(sIN.h, w_ga)
                                ins = e.tensor_scalar_mul(
                                    bk[0][:, :], gath[:, 0:128],
                                    coefrep[:, h:h + 1])
                                for kp in range(1, NCb):
                                    ins = e.scalar_tensor_tensor(
                                        bk[kp % 2][:, :],
                                        gath[:, kp * 128:(kp + 1) * 128],
                                        coefrep[:, kp * NH + h:kp * NH + h + 1],
                                        bk[(kp + 1) % 2][:, :], ALU.mult, ALU.add)
                                sV.inc(ins)
                                e.wait_ge(sT.h, sbf)
                                sa_, so_ = bk[(NCb - 1) % 2], bk[NCb % 2]
                                for c in range(NCH):
                                    col = c * NH + h
                                    e.tensor_copy(
                                        sinb_all[:, c * 128:(c + 1) * 128],
                                        sa_[:, :])
                                    ins = e.scalar_tensor_tensor(
                                        so_[:, :], sa_[:, :],
                                        grep[:, col:col + 1],
                                        sraw[:, c * 128:(c + 1) * 128],
                                        ALU.mult, ALU.add)
                                    sa_, so_ = so_, sa_
                                sV.inc(ins)
                            VEC(_)
                            w_scan = sV.n
                            gath_free = sV.n

                            for b in range(NB):
                                cols = [(4 * b + j) * NH + h for j in range(4)]
                                csl = [slice((4 * b + j) * CH, (4 * b + j + 1) * CH)
                                       for j in range(4)]

                                def _(e, cols=cols, cf=crep_free, ws=w_scan):
                                    e.wait_ge(sT.h, cf)
                                    e.wait_ge(sV.h, ws)
                                    for j in range(4):
                                        ins = e.activation(
                                            crep4[:, j * 128:(j + 1) * 128],
                                            onesf[:, :], AF.Identity,
                                            scale=ac_tm[:, cols[j]:cols[j] + 1])
                                    sS.inc(ins)
                                SCA(_)
                                w_cr = sS.n

                                def _(e, w_cr=w_cr, prtf=prt_free):
                                    e.wait_ge(sS.h, max(w_cr, prtf))
                                    for j in range(4):
                                        ins = e.transpose(
                                            p_rt[:, j * 128:(j + 1) * 128],
                                            crep4[:, j * 128:(j + 1) * 128],
                                            identf[:, :])
                                    sT.inc(ins)
                                TEN(_)
                                crep_free = sT.n
                                w_rt = sT.n

                                def _(e, cols=cols, w_rt=w_rt, tef=texp_free):
                                    e.wait_ge(sT.h, w_rt)
                                    e.wait_ge(sV.h, tef)
                                    for j in range(4):
                                        e.activation(
                                            rr4[:, j * 128:(j + 1) * 128],
                                            p_rt[:, j * 128:(j + 1) * 128],
                                            AF.Relu,
                                            bias=ac_tm[:, cols[j]:cols[j] + 1],
                                            scale=-1.0)
                                    for j in range(4):
                                        ins = e.activation(
                                            texp4[:, j * 128:(j + 1) * 128],
                                            rr4[:, j * 128:(j + 1) * 128],
                                            AF.Exp, scale=-1.0)
                                    sS.inc(ins)
                                SCA(_)
                                prt_free = sS.n
                                w_te = sS.n

                                def _(e, cols=cols, csl=csl, w_te=w_te,
                                      mtf=mt_free):
                                    e.wait_ge(sS.h, w_te)
                                    e.wait_ge(sT.h, mtf)
                                    for j in range(4):
                                        ins = e.scalar_tensor_tensor(
                                            mt4[:, j * 128:(j + 1) * 128],
                                            texp4[:, j * 128:(j + 1) * 128],
                                            dtsp_tm[:, cols[j]:cols[j] + 1],
                                            gmt[:, csl[j]], ALU.mult, ALU.mult)
                                    sV.inc(ins)
                                VEC(_)
                                texp_free = sV.n
                                w_mt = sV.n

                                def _(e, b=b, csl=csl, x_tm=x_tm, w_mt=w_mt,
                                      pyf_=pyy_free):
                                    e.wait_ge(sV.h, max(w_mt, pyf_))
                                    for j in range(4):
                                        c = 4 * b + j
                                        e.matmul(p_y[:, j * 128:(j + 1) * 128],
                                                 mt4[:, j * 128:(j + 1) * 128],
                                                 x_tm[:, c * 128:(c + 1) * 128],
                                                 start=True, stop=True)
                                    for j in range(4):
                                        c = 4 * b + j
                                        ins = e.matmul(
                                            p_yo[:, j * 128:(j + 1) * 128],
                                            c_fm[:, csl[j]],
                                            sinb_all[:, c * 128:(c + 1) * 128],
                                            start=True, stop=True)
                                    sT.inc(ins)
                                TEN(_)
                                mt_free = sT.n
                                last_mm_mark = sT.n
                                w_ty = sT.n

                                def _(e, cols=cols, w_ty=w_ty, ymf=ym_free):
                                    e.wait_ge(sT.h, max(w_ty, ymf))
                                    for j in range(4):
                                        e.tensor_scalar_mul(
                                            yt4[:, j * 128:(j + 1) * 128],
                                            p_yo[:, j * 128:(j + 1) * 128],
                                            e_tm[:, cols[j]:cols[j] + 1])
                                    for j in range(4):
                                        ins = e.tensor_add(
                                            ym4[:, j * 128:(j + 1) * 128],
                                            yt4[:, j * 128:(j + 1) * 128],
                                            p_y[:, j * 128:(j + 1) * 128])
                                    sV.inc(ins)
                                VEC(_)
                                pyy_free = sV.n
                                w_vy = sV.n

                                def _(e, w_vy=w_vy, pyff=pyf_free):
                                    e.wait_ge(sV.h, max(w_vy, pyff))
                                    for j in range(4):
                                        ins = e.transpose(
                                            p_yf[:, j * 128:(j + 1) * 128],
                                            ym4[:, j * 128:(j + 1) * 128],
                                            identf[:, :])
                                    sT.inc(ins)
                                TEN(_)
                                ym_free = sT.n
                                w_tf = sT.n

                                def _(e, h=h, csl=csl, xsb=xsb, zsb=zsb,
                                      w_tf=w_tf):
                                    e.wait_ge(sT.h, w_tf)
                                    for j in range(4):
                                        e.scalar_tensor_tensor(
                                            tf4[:, j * 128:(j + 1) * 128],
                                            xsb[:, csl[j]], drep[:, h:h + 1],
                                            p_yf[:, j * 128:(j + 1) * 128],
                                            ALU.mult, ALU.add)
                                    for j in range(4):
                                        ins = e.tensor_mul(
                                            g_all[:, csl[j]],
                                            tf4[:, j * 128:(j + 1) * 128],
                                            zsb[:, csl[j]])
                                    sV.inc(ins)
                                VEC(_)
                                pyf_free = sV.n
                            sinb_free = last_mm_mark

                            w_g = sV.n

                            def _(e, w_g=w_g):
                                e.wait_ge(sV.h, w_g)
                                sS.inc(e.activation(g2b[:, :], g_all[:, :],
                                                    AF.Square))
                            SCA(_)
                            w_g2 = sS.n

                            for gi in range(n512):
                                def _(e, gi=gi, w_g2=w_g2, psgf=psg_free):
                                    e.wait_ge(sS.h, w_g2)
                                    e.wait_ge(sV.h, psgf)
                                    sT.inc(e.matmul(
                                        p_sg[:, 0:512], onesb[:, :],
                                        g2b[:, gi * 512:(gi + 1) * 512],
                                        start=True, stop=True))
                                TEN(_)
                                w_tg = sT.n

                                def _(e, gi=gi, h=h, w_tg=w_tg):
                                    e.wait_ge(sT.h, w_tg)
                                    if h == 0:
                                        ins = e.tensor_copy(
                                            sg2[:, gi * 512:(gi + 1) * 512],
                                            p_sg[0:1, 0:512])
                                    else:
                                        ins = e.tensor_add(
                                            sg2[:, gi * 512:(gi + 1) * 512],
                                            sg2[:, gi * 512:(gi + 1) * 512],
                                            p_sg[0:1, 0:512])
                                    sV.inc(ins)
                                VEC(_)
                                psg_free = sV.n

                            def _(e, h=h, w_g=w_g, sp=spill_prev):
                                e.wait_ge(sV.h, w_g)
                                e.wait_ge(sOUT.h, sp)
                                sV.inc(e.tensor_scalar_mul(gwb[:, :], g_all[:, :],
                                                           nwm[:, h:h + 1]))
                            VEC(_)
                            w_gw = sV.n

                            def _(e, h=h, w_gw=w_gw):
                                e.wait_ge(sV.h, w_gw)
                                sOUT.inc(e.dma_start(out=gw_d[h], in_=gwb[:, :]))
                                if DEV_DUMPS and h == 0:
                                    sOUT.inc(e.dma_start(out=dbg_gw0, in_=gwb[:, :]))
                            SCA(_)
                            spill_prev = sOUT.n
                            xz_free[pb] = [sV.n, last_mm_mark]

                        if hh + 1 < NH:
                            hn = hh + 1
                            pbn = hn % 2

                            def _(e, hn=hn, pbn=pbn, xz=tuple(xz_free[pbn]),
                                  sp=spill_prev):
                                e.wait_ge(sV.h, xz[0])
                                e.wait_ge(sT.h, xz[1])
                                e.wait_ge(sOUT.h, sp)
                                sIN.inc(e.dma_start(out=xsb2[pbn][:, :],
                                                    in_=xsil_d[hn]))
                                sIN.inc(e.dma_start(out=zsb2[pbn][:, :],
                                                    in_=zsil_d[hn]))
                            SY(_)
                            xz_ld[hn] = sIN.n

        with nc.psum_tensor("p_rr", [128, 512], F32) as p_rr:
            w_sg = sV.n

            def _(e, w_sg=w_sg):
                e.wait_ge(sV.h, w_sg)
                sV.inc(e.tensor_scalar(rr_row[:, :], sg2[:, :], 1.0 / DI, EPS,
                                       ALU.mult, ALU.add))
            VEC(_)
            w_r1 = sV.n

            def _(e, w_r1=w_r1):
                e.wait_ge(sV.h, w_r1)
                sS.inc(e.activation(rr_row[:, :], rr_row[:, :], AF.Sqrt))
            SCA(_)
            w_r2 = sS.n

            def _(e, w_r2=w_r2):
                e.wait_ge(sS.h, w_r2)
                sV.inc(e.reciprocal(rr_row[:, :], rr_row[:, :]))
            VEC(_)
            w_r3 = sV.n
            if DEV_DUMPS:
                def _(e, w_sg2=sV.n):
                    e.wait_ge(sV.h, w_sg2)
                    sOUT.inc(e.dma_start(out=dbg_sg2, in_=sg2[:, :]))
                SCA(_)

        x1n_fm = sbuf("x1n_fm", [128, KD * T], BF16)

        with ExitStack() as ws:
            def wsb(name, shape, dt):
                return ws.enter_context(nc.sbuf_tensor(name, shape, dt))

            rrb = wsb("rrb", [128, T], F32)
            gws = wsb("gws", [128, 32 * 512], BF16)
            wob = [wsb("wob0", [128, 32 * 128], BF16),
                   wsb("wob1", [128, 32 * 128], BF16)]
            aw = wsb("aw", [128, 512], F32)
            u_ts = [wsb(f"u_ts{q}", [128, D], F32) for q in range(4)]
            bnb2 = wsb("bnb2", [128, 24], F32)
            bna2 = wsb("bna2", [128, 2], F32)
            rstd2 = wsb("rstd2", [128, 1], F32)
            nmr2 = wsb("nmr2", [128, 1], F32)
            lnt2 = [wsb("lnt2a", [128, D], BF16), wsb("lnt2b", [128, D], BF16)]

            with nc.psum_tensor("p_w", [128, 512], F32) as p_w, \
                 nc.psum_tensor("p_t2", [128, 512], F32) as p_t2, \
                 nc.psum_tensor("p_t2b", [128, 512], BF16) as p_t2b:
                for gi in range(n512):
                    def _(e, gi=gi, w_r3=w_r3, pwf=sV.n):
                        e.wait_ge(sV.h, max(w_r3, pwf))
                        sT.inc(e.matmul(p_w[:, 0:512], onesf[0:1, :],
                                        rr_row[0:1, gi * 512:(gi + 1) * 512],
                                        start=True, stop=True))
                    TEN(_)
                    w_tb = sT.n

                    def _(e, gi=gi, w_tb=w_tb):
                        e.wait_ge(sT.h, w_tb)
                        sV.inc(e.tensor_copy(rrb[:, gi * 512:(gi + 1) * 512],
                                             p_w[:, 0:512]))
                    VEC(_)
                rr_done = sV.n

                def _(e, bt=sT.n, bo=sOUT.n):
                    e.wait_ge(sT.h, bt)
                    e.wait_ge(sOUT.h, bo)
                    sW.inc(e.dma_start(out=wob[0][:, :], in_=w_out[0]))
                GP(_)

                pw_free = rr_done
                pt2_free = sV.n
                wo_mm_end = {}
                prev_tg_mm = sT.n
                lnt2_free = [sT.n, sT.n]
                prev_norm2_s = sS.n
                for tgi in range(n512):
                    g0 = tgi * 512
                    def _(e, g0=g0, sp=sOUT.n, pm=prev_tg_mm):
                        e.wait_ge(sOUT.h, sp)
                        e.wait_ge(sT.h, pm)
                        for kk in range(32):
                            sIN.inc(e.dma_start(out=gws[:, kk * 512:(kk + 1) * 512],
                                                in_=gw_d[kk][:, g0:g0 + 512]))
                        for q in range(4):
                            r0 = CTX + g0 + q * 128
                            sIN.inc(e.dma_start(out=u_ts[q][:, :],
                                                in_=u_in[r0:r0 + 128, :]))
                    SY(_)
                    w_gws = sIN.n

                    for o in range(KD):
                        bi = tgi * KD + o
                        buf = wob[bi % 2]
                        w_wo = sW.n
                        if bi + 1 < n512 * KD:
                            o2 = (bi + 1) % KD
                            nbuf = wob[(bi + 1) % 2]

                            def _(e, o2=o2, nbuf=nbuf, wprev=wo_mm_end.get(bi - 1, 0)):
                                e.wait_ge(sT.h, wprev)
                                sW.inc(e.dma_start(out=nbuf[:, :], in_=w_out[o2]))
                            GP(_)

                        def _(e, o=o, buf=buf, w_wo=w_wo, w_gws=w_gws, pwf=pw_free):
                            e.wait_ge(sW.h, w_wo)
                            e.wait_ge(sIN.h, w_gws)
                            e.wait_ge(sV.h, pwf)
                            for kk in range(32):
                                ins = e.matmul(p_w[:, 0:512],
                                               buf[:, kk * 128:(kk + 1) * 128],
                                               gws[:, kk * 512:(kk + 1) * 512],
                                               start=(kk == 0), stop=(kk == 31))
                            sT.inc(ins)
                        TEN(_)
                        wo_mm_end[bi] = sT.n
                        w_mm = sT.n

                        def _(e, g0=g0, w_mm=w_mm):
                            e.wait_ge(sT.h, w_mm)
                            sV.inc(e.tensor_mul(aw[:, :], p_w[:, 0:512],
                                                rrb[:, g0:g0 + 512]))
                        VEC(_)
                        pw_free = sV.n
                        w_aw = sV.n

                        for q in range(4):
                            def _(e, q=q, w_aw=w_aw, ptf=pt2_free):
                                e.wait_ge(sV.h, max(w_aw, ptf))
                                sT.inc(e.transpose(p_t2[:, 0:128],
                                                   aw[:, q * 128:(q + 1) * 128],
                                                   identf[:, :]))
                            TEN(_)
                            w_tt2 = sT.n

                            def _(e, q=q, o=o, w_tt2=w_tt2):
                                e.wait_ge(sT.h, w_tt2)
                                sV.inc(e.tensor_add(
                                    u_ts[q][:, o * 128:(o + 1) * 128],
                                    u_ts[q][:, o * 128:(o + 1) * 128],
                                    p_t2[:, 0:128]))
                            VEC(_)
                            pt2_free = sV.n

                    x1_done = sV.n
                    for q in range(4):
                        it = tgi * 4 + q

                        def _(e, q=q, w_x1=x1_done):
                            e.wait_ge(sV.h, w_x1)
                            for p4 in range(4):
                                ins = e.bn_stats(bnb2[:, 6 * p4:6 * p4 + 6],
                                                 u_ts[q][:, 512 * p4:512 * (p4 + 1)])
                            sV.inc(ins)
                            e.wait_ge(sV.h, sV.n)
                        VEC(_)
                        emit_ln_aggr(bnb2, bna2, 128, wait_s=prev_norm2_s)
                        w_st2 = sV.n

                        def _(e, w_st2=w_st2):
                            e.wait_ge(sV.h, w_st2)
                            sS.inc(e.activation(rstd2[:, :], bna2[:, 1:2], AF.Sqrt,
                                                bias=EPS))
                        SCA(_)
                        w_sq2 = sS.n

                        def _(e, w_sq2=w_sq2):
                            e.wait_ge(sS.h, w_sq2)
                            sV.inc(e.reciprocal(rstd2[:, :], rstd2[:, :]))
                            e.wait_ge(sV.h, sV.n)
                            sV.inc(e.tensor_scalar(nmr2[:, :], bna2[:, 0:1],
                                                   rstd2[:, 0:1], -1.0,
                                                   ALU.mult, ALU.mult))
                        VEC(_)
                        w_rs2 = sV.n

                        def _(e, q=q, w_rs2=w_rs2, lf=lnt2_free[q % 2]):
                            e.wait_ge(sV.h, w_rs2)
                            e.wait_ge(sT.h, lf)
                            sS.inc(e.activation(lnt2[q % 2][:, :], u_ts[q][:, :],
                                                AF.Identity, bias=nmr2[:, 0:1],
                                                scale=rstd2[:, 0:1]))
                        SCA(_)
                        w_no2 = sS.n
                        prev_norm2_s = w_no2

                        def _(e, q=q, it=it, w_no2=w_no2):
                            e.wait_ge(sS.h, w_no2)
                            sOUT.inc(e.dma_start(out=x1_d[it], in_=u_ts[q][:, :]))
                            if DEV_DUMPS and it == 0:
                                sOUT.inc(e.dma_start(out=dbg_x1, in_=u_ts[0][:, :]))
                        SCA(_)

                        for k4 in range(0, KD, 4):
                            def _(e, k4=k4, w_no2=w_no2, ptf=pt2_free):
                                e.wait_ge(sS.h, w_no2)
                                e.wait_ge(sV.h, ptf)
                                for kk in range(k4, k4 + 4):
                                    ins = e.transpose(
                                        p_t2b[:, (kk - k4) * 128:(kk - k4 + 1) * 128],
                                        lnt2[q % 2][:, kk * 128:(kk + 1) * 128],
                                        identb[:, :])
                                sT.inc(ins)
                            TEN(_)
                            w_tt3 = sT.n

                            def _(e, k4=k4, it=it, w_tt3=w_tt3):
                                e.wait_ge(sT.h, w_tt3)
                                for kk in range(k4, k4 + 4):
                                    ins = e.tensor_copy(
                                        x1n_fm[:, kk * T + it * 128:
                                               kk * T + (it + 1) * 128],
                                        p_t2b[:, (kk - k4) * 128:(kk - k4) * 128 + 128])
                                sV.inc(ins)
                            VEC(_)
                            pt2_free = sV.n
                        lnt2_free[q % 2] = sT.n
                    prev_tg_mm = sT.n

        with ExitStack() as fs:
            def fsb(name, shape, dt):
                return fs.enter_context(nc.sbuf_tensor(name, shape, dt))

            h1_fm = fsb("h1_fm", [128, KD * T], BF16)
            wfb = [fsb("wfb0", [128, KD * 128], BF16),
                   fsb("wfb1", [128, KD * 128], BF16)]
            web = [fsb("web0", [128, KD * 128], BF16),
                   fsb("web1", [128, KD * 128], BF16)]
            ff = fsb("ff", [128, 512], F32)
            x1r = [fsb(f"x1r{i}", [128, 128], F32) for i in range(4)]
            ot = [fsb("ot0", [128, 128], F32), fsb("ot1", [128, 128], F32)]

            with nc.psum_tensor("p_f", [128, 512], F32) as p_f, \
                 nc.psum_tensor("p_t3", [128, 512], F32) as p_t3:
                def _(e, bt=sT.n, bv=sV.n, bs=sS.n, bo=sOUT.n):
                    e.wait_ge(sT.h, bt)
                    e.wait_ge(sV.h, bv)
                    e.wait_ge(sS.h, bs)
                    e.wait_ge(sOUT.h, bo)
                    sW.inc(e.dma_start(out=wfb[0][:, :], in_=w_up[0]))
                GP(_)
                x1n_ready = sV.n
                pf_free = sS.n
                f1_mm_end = {}
                for o in range(KD):
                    buf = wfb[o % 2]
                    w_wf = sW.n
                    if o + 1 < KD:
                        def _(e, o=o, wprev=f1_mm_end.get(o - 1, 0)):
                            e.wait_ge(sT.h, wprev)
                            sW.inc(e.dma_start(out=wfb[(o + 1) % 2][:, :],
                                               in_=w_up[o + 1]))
                        GP(_)
                    for tgi in range(n512):
                        g0 = tgi * 512

                        def _(e, o=o, g0=g0, buf=buf, w_wf=w_wf, pff=pf_free,
                              xr=x1n_ready):
                            e.wait_ge(sW.h, w_wf)
                            e.wait_ge(sS.h, pff)
                            e.wait_ge(sV.h, xr)
                            for kk in range(KD):
                                ins = e.matmul(p_f[:, 0:512],
                                               buf[:, kk * 128:(kk + 1) * 128],
                                               x1n_fm[:, kk * T + g0:kk * T + g0 + 512],
                                               start=(kk == 0), stop=(kk == KD - 1))
                            sT.inc(ins)
                        TEN(_)
                        f1_mm_end[o] = sT.n
                        w_f1 = sT.n

                        def _(e, o=o, g0=g0, w_f1=w_f1):
                            e.wait_ge(sT.h, w_f1)
                            sS.inc(e.activation(h1_fm[:, o * T + g0:o * T + g0 + 512],
                                                p_f[:, 0:512], AF.Relu))
                        SCA(_)
                        pf_free = sS.n

                h1_ready = sS.n

                def _(e, hr=h1_ready):
                    e.wait_ge(sS.h, hr)
                    sW.inc(e.dma_start(out=web[0][:, :], in_=w_ex[0]))
                    sW.inc(e.dma_start(out=wfb[0][:, :], in_=w_dn[0]))
                GP(_)
                f2_mm_end = {}
                pf2_free = sV.n
                pt3_free = sV.n
                x1_ready = sOUT.n
                x1r_free = [sV.n] * 4
                ot_free = [sOUT.n, sOUT.n]
                for dp in range(KD):
                    w_wf2 = sW.n
                    ebuf = web[dp % 2]
                    dbuf = wfb[dp % 2]
                    if dp + 1 < KD:
                        def _(e, dp=dp, wprev=f2_mm_end.get(dp - 1, 0)):
                            e.wait_ge(sT.h, wprev)
                            sW.inc(e.dma_start(out=web[(dp + 1) % 2][:, :],
                                               in_=w_ex[dp + 1]))
                            sW.inc(e.dma_start(out=wfb[(dp + 1) % 2][:, :],
                                               in_=w_dn[dp + 1]))
                        GP(_)
                    for tgi in range(n512):
                        g0 = tgi * 512

                        def _(e, dp=dp, tgi=tgi, xr=x1_ready, xf=tuple(x1r_free)):
                            e.wait_ge(sOUT.h, xr)
                            for q in range(4):
                                e.wait_ge(sV.h, xf[q])
                                it = tgi * 4 + q
                                sIN.inc(e.dma_start(
                                    out=x1r[q][:, :],
                                    in_=x1_d[it][:, dp * 128:(dp + 1) * 128]))
                        SY(_)
                        w_x1r_all = sIN.n

                        def _(e, dp=dp, g0=g0, ebuf=ebuf, dbuf=dbuf, w_wf2=w_wf2,
                              pff=pf2_free, hr=h1_ready):
                            e.wait_ge(sW.h, w_wf2)
                            e.wait_ge(sS.h, hr)
                            e.wait_ge(sV.h, pff)
                            for kk in range(KD):
                                e.matmul(p_f[:, 0:512],
                                         ebuf[:, kk * 128:(kk + 1) * 128],
                                         x1n_fm[:, kk * T + g0:kk * T + g0 + 512],
                                         start=(kk == 0), stop=False)
                            for kk in range(KD):
                                ins = e.matmul(p_f[:, 0:512],
                                               dbuf[:, kk * 128:(kk + 1) * 128],
                                               h1_fm[:, kk * T + g0:kk * T + g0 + 512],
                                               start=False, stop=(kk == KD - 1))
                            sT.inc(ins)
                        TEN(_)
                        f2_mm_end[dp] = sT.n
                        w_f2 = sT.n

                        def _(e, w_f2=w_f2):
                            e.wait_ge(sT.h, w_f2)
                            sV.inc(e.tensor_copy(ff[:, :], p_f[:, 0:512]))
                        VEC(_)
                        pf2_free = sV.n
                        w_ff = sV.n

                        for q in range(4):
                            it = tgi * 4 + q
                            qb = q % 2

                            def _(e, q=q, w_ff=w_ff, ptf=pt3_free):
                                e.wait_ge(sV.h, max(w_ff, ptf))
                                sT.inc(e.transpose(p_t3[:, 0:128],
                                                   ff[:, q * 128:(q + 1) * 128],
                                                   identf[:, :]))
                            TEN(_)
                            w_tr3 = sT.n

                            def _(e, q=q, qb=qb, w_tr3=w_tr3, w_x1r=w_x1r_all,
                                  of=ot_free[qb]):
                                e.wait_ge(sT.h, w_tr3)
                                e.wait_ge(sIN.h, w_x1r)
                                e.wait_ge(sOUT.h, of)
                                sV.inc(e.tensor_add(ot[qb][:, :], x1r[q][:, :],
                                                    p_t3[:, 0:128]))
                            VEC(_)
                            pt3_free = sV.n
                            x1r_free[q] = sV.n
                            w_ot = sV.n

                            def _(e, it=it, dp=dp, qb=qb, w_ot=w_ot):
                                e.wait_ge(sV.h, w_ot)
                                sOUT.inc(e.dma_start(
                                    out=out[it * 128:(it + 1) * 128,
                                            dp * 128:(dp + 1) * 128],
                                    in_=ot[qb][:, :]))
                            SCA(_)
                            ot_free[qb] = sOUT.n

            if DEV_DUMPS:
                def _(e, w=sT.n):
                    e.wait_ge(sT.h, w)
                    sOUT.inc(e.dma_start(out=dbg_x1n, in_=x1n_fm[:, :]))
                    sOUT.inc(e.dma_start(out=dbg_h1, in_=h1_fm[:, :]))
                SCA(_)

        def _(e, fin=sOUT.n):
            e.wait_ge(sOUT.h, fin)
        GP(_)

    return nc


def _prep_inputs(inputs, T, n_cores):
    u = np.asarray(inputs["L0_attn_u"])[0]
    W_in = np.asarray(inputs["L0_attn_W_in"])
    W_conv = np.asarray(inputs["L0_attn_W_conv"])
    b_conv = np.asarray(inputs["L0_attn_b_conv"])
    dt_bias = np.asarray(inputs["L0_attn_dt_bias"])
    A_log = np.asarray(inputs["L0_attn_A_log"])
    D_sk = np.asarray(inputs["L0_attn_D"])
    norm_w = np.asarray(inputs["L0_attn_norm_w"])
    W_out = np.asarray(inputs["L0_attn_W_out"])
    W_up = np.asarray(inputs["L0_ffn_W_up"])
    W_dn = np.asarray(inputs["L0_ffn_W_down"])
    W_ex = np.asarray(inputs["L0_ffn_W_expert"])

    L = u.shape[0]
    assert L == T * n_cores

    def blk_lhsT(W, R):
        nb = W.shape[0] // R
        a = W.reshape(nb, R, W.shape[1] // 128, 128)
        return np.ascontiguousarray(a.transpose(0, 3, 2, 1)).reshape(
            nb, 128, -1)

    wz = blk_lhsT(W_in[0:DI], 128)
    wxx = blk_lhsT(W_in[DI:2 * DI], 128)
    wb = blk_lhsT(W_in[2 * DI:2 * DI + DS], DS)[0]
    wc = blk_lhsT(W_in[2 * DI + DS:2 * DI + 2 * DS], DS)[0]
    wdt = blk_lhsT(W_in[2 * DI + 2 * DS:], NH)[0]
    wo = blk_lhsT(W_out, 128)
    wu = blk_lhsT(W_up, 128)
    wd = blk_lhsT(W_dn, 128)
    we = np.ascontiguousarray(
        W_ex.reshape(KD, 128, KD, 128).transpose(2, 1, 0, 3)).reshape(KD, 128, -1)

    jj, ii = np.meshgrid(np.arange(128), np.arange(128), indexing="ij")
    le = (jj <= ii).astype(np.float32)
    ident = np.eye(128, dtype=np.float32)
    sel = np.zeros((n_cores, n_cores * DS), np.float32)
    for kp in range(n_cores):
        sel[kp, kp * DS:(kp + 1) * DS] = 1.0
    small = np.stack([dt_bias, A_log, D_sk, np.zeros(NH, np.float32)],
                     axis=1).astype(np.float32)
    nwm = norm_w.reshape(NH, 128).T.copy()
    bcx_ = b_conv[0:DI].reshape(NH, 128).T.copy()
    bcbc_ = np.stack([b_conv[DI:DI + DS], b_conv[DI + DS:DI + 2 * DS]], axis=1)
    tpx_ = W_conv[0:DI, 0, :].reshape(NH, 128, 4).transpose(1, 0, 2).reshape(128, -1)
    tpbc_ = np.concatenate([W_conv[DI:DI + DS, 0, :],
                            W_conv[DI + DS:DI + 2 * DS, 0, :]], axis=1)

    import ml_dtypes
    bf = lambda x: np.ascontiguousarray(x).astype(ml_dtypes.bfloat16)

    shared = dict(
        w_z=np.ascontiguousarray(wz), w_x=np.ascontiguousarray(wxx),
        w_b=np.ascontiguousarray(wb), w_c=np.ascontiguousarray(wc),
        w_dt=np.ascontiguousarray(wdt), w_out=np.ascontiguousarray(wo),
        w_up=np.ascontiguousarray(wu), w_dn=np.ascontiguousarray(wd),
        w_ex=np.ascontiguousarray(we),
        c_identf=ident, c_identb=bf(ident), c_le=le, c_leb=bf(le),
        c_ones=np.ones((128, 128), np.float32),
        c_onesb=bf(np.ones((128, 128), np.float32)), c_sel=sel,
        c_small=small, c_drow=np.ascontiguousarray(D_sk[None, :].astype(np.float32)),
        c_fvals=np.tile(np.array([[EPS, 1.0, -1.0, 0.0, 1.0 / DI, 0.0, 0.0, 0.0]],
                        np.float32), (128, 1)),
        c_nw=np.ascontiguousarray(nwm),
        c_bcx=np.ascontiguousarray(bcx_), c_bcbc=np.ascontiguousarray(bcbc_),
        c_tpx=np.ascontiguousarray(tpx_), c_tpbc=np.ascontiguousarray(tpbc_),
    )
    in_maps = []
    for k in range(n_cores):
        s0 = k * T
        if k == 0:
            ctx = np.zeros((CTX, D), np.float32)
        else:
            ctx = u[s0 - CTX:s0]
        ue = np.ascontiguousarray(np.concatenate([ctx, u[s0:s0 + T]], axis=0))
        qm = np.zeros((n_cores, n_cores), np.float32)
        qm[:k, :] = 1.0
        cm = np.zeros((n_cores, 1), np.float32)
        cm[:k] = 1.0
        m = dict(shared)
        m.update(u_in=ue.astype(np.float32), c_qmask=qm, c_cmask=cm)
        in_maps.append(m)
    return in_maps


def kernel(**inputs):
    n_cores = 8
    L = np.asarray(inputs["L0_attn_u"]).shape[1]
    T = L // n_cores
    in_maps = _prep_inputs(inputs, T, n_cores)
    nc = build(T, n_cores)
    res = run_bass_kernel_spmd(nc, in_maps, core_ids=list(range(n_cores)))
    outs = [np.asarray(res.results[k]["out"], dtype=np.float32)
            for k in range(n_cores)]
    return np.concatenate(outs, axis=0)[None]
